# version 01
# baseline (speedup 1.0000x reference)
import contextlib
import numpy as np
import concourse.bass as bass
import concourse.mybir as mybir
from concourse.bass_utils import run_bass_kernel_spmd

F32 = mybir.dt.float32
AF = mybir.ActivationFunctionType
ALU = mybir.AluOpType
AX = mybir.AxisListType

NCORE = 8
NTOK = 16384
TPC = NTOK // NCORE
DM = 1024

_EYE = np.eye(128, dtype=np.float32)
_ONES_ROW = np.ones((1, 128), dtype=np.float32)


def T(a):
    return np.ascontiguousarray(np.asarray(a).T)


class Prog:
    def __init__(self, name):
        self.name = name
        self.nc = bass.Bass("TRN2", target_bir_lowering=False)
        self.ops = []
        self._cum = 0
        self.finished = False
        self.consts = {}

    def cf(self, v):
        v = float(v)
        if v not in self.consts:
            self.consts[v] = len(self.consts)
        return v

    def din(self, name, shape):
        return self.nc.dram_tensor(name, list(shape), F32, kind="ExternalInput").ap()

    def dout(self, name, shape):
        return self.nc.dram_tensor(name, list(shape), F32, kind="ExternalOutput").ap()

    def sbuf(self, name, shape):
        return self.nc.alloc_sbuf_tensor(name, list(shape), F32).ap()

    def psum(self, name, shape):
        return self.nc.alloc_psum_tensor(name, list(shape), F32).ap()

    def op(self, eng, fn, inc=1, wait=None):
        self.ops.append((eng, fn, inc, wait))
        self._cum += inc

    def mark(self):
        return self._cum

    def dma(self, eng, dst, src, wait=None):
        self.op(eng, lambda e, d=dst, s=src: e.dma_start(out=d, in_=s), 16, wait)

    def finish(self):
        assert not self.finished
        self.finished = True
        nc = self.nc
        ncst = len(self.consts)
        if ncst:
            cpool = nc.alloc_sbuf_tensor("cpool_", [128, ncst], F32).ap()
            cin = nc.dram_tensor("consts_", [128, ncst], F32,
                                 kind="ExternalInput").ap()
            for v, i in self.consts.items():
                nc.const_aps.register(F32, v, cpool[:, i:i + 1])
            self.ops.insert(
                0, ("sync", lambda e, d=cpool, s=cin: e.dma_start(out=d, in_=s),
                    16, None))
            arr = np.zeros((128, ncst), np.float32)
            for v, i in self.consts.items():
                arr[:, i] = v
            self.const_arr = arr
        else:
            self.const_arr = None
        shift = 16 if ncst else 0
        sched = []
        cum = 0
        for (eng, fn, inc, wait) in self.ops:
            w = cum if wait is None else wait + shift
            sched.append((eng, fn, inc, w))
            cum += inc
        total = cum
        used = {e for (e, _, _, _) in sched} | {"vector"}
        self._stack = contextlib.ExitStack()
        sem = self._stack.enter_context(nc.semaphore("tick"))
        blk = self._stack.enter_context(nc.Block())

        def mk(name):
            def emit(e):
                for (eng, fn, inc, w) in sched:
                    if eng != name:
                        continue
                    if w > 0:
                        e.wait_ge(sem, w)
                    inst = fn(e)
                    inst.then_inc(sem, inc)
                if name == "vector":
                    e.wait_ge(sem, total)
            return emit

        for ename in ("sync", "tensor", "scalar", "vector", "gpsimd"):
            if ename in used:
                getattr(blk, ename)(mk(ename))
        self._stack.close()
        return nc


_PROG_CACHE = {}


def cached(key, builder):
    if key not in _PROG_CACHE:
        p = builder()
        p.finish()
        _PROG_CACHE[key] = p
    return _PROG_CACHE[key]


def run(prog, in_maps):
    if prog.const_arr is not None:
        for im in in_maps:
            im["consts_"] = prog.const_arr
    res = run_bass_kernel_spmd(prog.nc, in_maps, core_ids=list(range(NCORE)))
    return res.results


def act(p, out, in_, func, bias=0.0, scale=1.0):
    if isinstance(bias, float) and bias != 0.0:
        p.cf(bias)
    if isinstance(scale, float) and scale != 1.0:
        p.cf(scale)
    p.op("scalar", lambda e, o=out, i=in_, f=func, b=bias, s=scale:
         e.activation(out=o, in_=i, func=f, bias=b, scale=s))


def vop(p, name, *args):
    p.op("vector", lambda e, n=name, a=args: getattr(e, n)(*a))


def tt(p, out, a, b, opname):
    p.op("vector", lambda e, o=out, x=a, y=b, op=getattr(ALU, opname):
         e.tensor_tensor(out=o, in0=x, in1=y, op=op))


def ts(p, out, a, s1, opname, s2=None, op2name="bypass"):
    p.op("vector", lambda e, o=out, x=a, s=s1, t=s2, op=getattr(ALU, opname),
         op2=getattr(ALU, op2name):
         e.tensor_scalar(out=o, in0=x, scalar1=s, scalar2=t, op0=op, op1=op2))


def red(p, out, in_, opname, absval=False):
    p.op("vector", lambda e, o=out, i=in_, op=getattr(ALU, opname), av=absval:
         e.tensor_reduce(out=o, in_=i, axis=AX.X, op=op,
                         apply_absolute_value=av))


def mm(p, out, lhsT, rhs, start, stop):
    p.op("tensor", lambda e, o=out, l=lhsT, r=rhs, st=start, sp=stop:
         e.matmul(out=o, lhsT=l, rhs=r, start=st, stop=sp,
                  skip_group_check=True))


def trans(p, out_psum, in_sbuf, eye_sbuf):
    p.op("tensor", lambda e, o=out_psum, i=in_sbuf, y=eye_sbuf:
         e.transpose(out=o, in_=i, identity=y))


def copy(p, out, in_):
    p.op("vector", lambda e, o=out, i=in_: e.tensor_copy(out=o, in_=i))


def scopy(p, out, in_, scale=1.0):
    act(p, out, in_, AF.Identity, bias=0.0, scale=scale)


def memset(p, ap_, val=0.0):
    p.op("vector", lambda e, a=ap_, v=val: e.memset(a, v))


def build_norm(kind, Dd=DM, eps=1e-5):
    def b():
        p = Prog(f"{kind}{Dd}")
        x = p.din("x", [TPC, Dd])
        w = p.din("w", [128, Dd]) if kind == "rms" else None
        y = p.dout("y", [TPC, Dd])
        xt = p.sbuf("xt", [128, Dd])
        wt = p.sbuf("wt", [128, Dd]) if kind == "rms" else None
        m = p.sbuf("m", [128, 1])
        v = p.sbuf("v", [128, 1])
        r = p.sbuf("r", [128, 1])
        sq = p.sbuf("sq", [128, Dd])
        if kind == "rms":
            p.dma("sync", wt, w)
        for t in range(TPC // 128):
            p.dma("sync", xt, x[t * 128:(t + 1) * 128, :])
            if kind == "ln":
                red(p, m, xt, "add")
                ts(p, m, m, 1.0 / Dd, "mult")
                ts(p, xt, xt, m, "subtract")
            act(p, sq, xt, AF.Square)
            red(p, v, sq, "add")
            ts(p, v, v, 1.0 / Dd, "mult")
            act(p, r, v, AF.Sqrt, bias=eps)
            vop(p, "reciprocal", r, r)
            ts(p, xt, xt, r, "mult")
            if kind == "rms":
                tt(p, xt, xt, wt, "mult")
            p.dma("gpsimd", y[t * 128:(t + 1) * 128, :], xt)
        return p
    return cached(("norm", kind, Dd, eps), b)


def launch_norm(kind, x_nat, w=None, eps=1e-5):
    Dd = x_nat.shape[1]
    p = build_norm(kind, Dd, eps)
    ims = []
    for c in range(NCORE):
        im = {"x": np.ascontiguousarray(x_nat[c * TPC:(c + 1) * TPC])}
        if kind == "rms":
            im["w"] = np.ascontiguousarray(
                np.broadcast_to(np.asarray(w)[None, :], (128, Dd)))
        ims.append(im)
    rs = run(p, ims)
    return np.concatenate([r["y"] for r in rs], 0)


def build_gemm(K, M, actname, has_bias, bin_, elu1_blocks=0):
    nkc = (K + 127) // 128
    nmb = (M + 127) // 128
    TSEG = 512
    ntseg = TPC // TSEG
    resident = (K * M * 4) <= 12 * 2 ** 20

    def b():
        p = Prog(f"gemm{K}x{M}{actname}")
        xT = p.din("xT", [K, TPC])
        wT = p.din("wT", [K, M])
        bias = p.din("bias", [128, nmb]) if has_bias else None
        wnat = p.din("wnat", [M, K]) if bin_ else None
        out = p.dout("out", [M, TPC])

        xbuf = p.sbuf("xbuf", [128, nkc * TSEG])
        obuf = p.sbuf("obuf", [128, TSEG])
        t1 = p.sbuf("t1", [128, TSEG])
        t2 = p.sbuf("t2", [128, TSEG])
        ps = p.psum("ps", [128, TSEG])
        bt = None
        if has_bias:
            bt = p.sbuf("bt", [128, nmb])
            p.dma("sync", bt, bias)
        alph = None
        if bin_:
            alph = p.sbuf("alph", [128, nmb])
            wrow = p.sbuf("wrow", [128, K])
            for mb in range(nmb):
                mrows = min(128, M - mb * 128)
                p.dma("sync", wrow[:mrows, :], wnat[mb * 128:mb * 128 + mrows, :])
                red(p, alph[:mrows, mb:mb + 1], wrow[:mrows, :], "add", absval=True)
                ts(p, alph[:mrows, mb:mb + 1], alph[:mrows, mb:mb + 1],
                   1.0 / K, "mult")
        if resident:
            wsb = p.sbuf("wsb", [128, nkc * M])
            dq = ["sync", "gpsimd", "scalar"]
            wv = p.mark()
            for kc in range(nkc):
                krows = min(128, K - kc * 128)
                p.dma(dq[kc % 3], wsb[:krows, kc * M:kc * M + M],
                      wT[kc * 128:kc * 128 + krows, :], wait=wv)
            if bin_:
                for kc in range(nkc):
                    krows = min(128, K - kc * 128)
                    act(p, wsb[:krows, kc * M:(kc + 1) * M],
                        wsb[:krows, kc * M:(kc + 1) * M], AF.Sign)
        else:
            wcol = p.sbuf("wcol", [128, nkc * 128])

        for tg in range(ntseg):
            wv = p.mark()
            for kc in range(nkc):
                krows = min(128, K - kc * 128)
                p.dma("sync" if kc % 2 == 0 else "gpsimd",
                      xbuf[:krows, kc * TSEG:(kc + 1) * TSEG],
                      xT[kc * 128:kc * 128 + krows, tg * TSEG:(tg + 1) * TSEG],
                      wait=wv)
            for mb in range(nmb):
                mrows = min(128, M - mb * 128)
                if not resident:
                    wv = p.mark()
                    dq = ["sync", "gpsimd", "scalar"]
                    for kc in range(nkc):
                        krows = min(128, K - kc * 128)
                        p.dma(dq[kc % 3], wcol[:krows, kc * 128:kc * 128 + mrows],
                              wT[kc * 128:kc * 128 + krows,
                                 mb * 128:mb * 128 + mrows], wait=wv)
                    if bin_:
                        act(p, wcol[:, :], wcol[:, :], AF.Sign)
                for kc in range(nkc):
                    krows = min(128, K - kc * 128)
                    if resident:
                        lhs = wsb[:krows, kc * M + mb * 128:kc * M + mb * 128 + mrows]
                    else:
                        lhs = wcol[:krows, kc * 128:kc * 128 + mrows]
                    mm(p, ps[:mrows, :], lhs, xbuf[:krows, kc * TSEG:(kc + 1) * TSEG],
                       kc == 0, kc == nkc - 1)
                src = ps
                if bin_:
                    ts(p, obuf[:mrows, :], ps[:mrows, :], alph[:mrows, mb:mb + 1],
                       "mult", bt[:mrows, mb:mb + 1] if has_bias else 0.0,
                       "add" if has_bias else "bypass")
                    src = obuf
                elif has_bias:
                    ts(p, obuf[:mrows, :], ps[:mrows, :], bt[:mrows, mb:mb + 1], "add")
                    src = obuf
                if actname == "none":
                    if src is ps:
                        copy(p, obuf[:mrows, :], ps[:mrows, :])
                elif actname == "relu":
                    act(p, obuf[:mrows, :], src[:mrows, :], AF.Relu)
                elif actname == "silu":
                    act(p, obuf[:mrows, :], src[:mrows, :], AF.Silu)
                elif actname == "gelu":
                    act(p, obuf[:mrows, :], src[:mrows, :], AF.Gelu)
                elif actname == "elu1":
                    if mb < elu1_blocks:
                        ts(p, t1[:mrows, :], src[:mrows, :], 0.0, "max")
                        ts(p, t2[:mrows, :], src[:mrows, :], 0.0, "min")
                        act(p, t2[:mrows, :], t2[:mrows, :], AF.Exp)
                        tt(p, obuf[:mrows, :], t1[:mrows, :], t2[:mrows, :], "add")
                    elif src is ps:
                        copy(p, obuf[:mrows, :], ps[:mrows, :])
                else:
                    raise ValueError(actname)
                p.dma("gpsimd", out[mb * 128:mb * 128 + mrows,
                                    tg * TSEG:(tg + 1) * TSEG], obuf[:mrows, :])
        return p
    return cached(("gemm", K, M, actname, has_bias, bin_, elu1_blocks), b)


def launch_gemm(xT_full, wT, actname="none", bias=None, wnat=None, elu1_blocks=0):
    K, M = wT.shape
    nmb = (M + 127) // 128
    p = build_gemm(K, M, actname, bias is not None, wnat is not None, elu1_blocks)
    wTc = np.ascontiguousarray(wT)
    bb = None
    if bias is not None:
        bb = np.zeros((128, nmb), np.float32)
        bv = np.asarray(bias)
        for mb in range(nmb):
            seg = bv[mb * 128:(mb + 1) * 128]
            bb[:len(seg), mb] = seg
    wn = np.ascontiguousarray(wnat) if wnat is not None else None
    ims = []
    for c in range(NCORE):
        im = {"xT": np.ascontiguousarray(xT_full[:, c * TPC:(c + 1) * TPC]),
              "wT": wTc}
        if bb is not None:
            im["bias"] = bb
        if wn is not None:
            im["wnat"] = wn
        ims.append(im)
    rs = run(p, ims)
    return np.concatenate([r["out"] for r in rs], 1)


def build_swa():
    S = 4096
    def b():
        p = Prog("swa")
        qT = p.din("qT", [8, 64, S])
        kT = p.din("kT", [8, 64, S + 256])
        vp = p.din("vp", [8, S + 256, 64])
        msk = p.din("msk", [3, 128, 384])
        eye = p.din("eye", [128, 128])
        o = p.dout("o", [8, S, 64])

        qs = p.sbuf("qs", [64, S])
        ks = p.sbuf("ks", [64, S + 256])
        vs = p.sbuf("vs", [128, 34 * 64])
        ms = p.sbuf("ms", [128, 3 * 384])
        eyes = p.sbuf("eyes", [128, 128])
        sc = p.sbuf("sc", [128, 384])
        mx = p.sbuf("mx", [128, 1])
        sm = p.sbuf("sm", [128, 1])
        pt = p.sbuf("pt", [128, 384])
        ob = p.sbuf("ob", [128, 64])
        ps = p.psum("ps", [128, 384])
        pt_ps = p.psum("pt_ps", [128, 128])
        ops_ = p.psum("ops", [128, 64])

        p.dma("sync", eyes, eye)
        for j in range(3):
            p.dma("sync", ms[:, j * 384:(j + 1) * 384], msk[j])
        for u in range(8):
            wv = p.mark()
            p.dma("sync", qs, qT[u], wait=wv)
            p.dma("gpsimd", ks, kT[u], wait=wv)
            for j in range(34):
                p.dma("scalar", vs[:, j * 64:(j + 1) * 64],
                      vp[u, j * 128:(j + 1) * 128, :], wait=wv)
            for t in range(32):
                mi = 0 if t == 0 else (2 if t == 31 else 1)
                mm(p, ps, qs[:, t * 128:(t + 1) * 128],
                   ks[:, t * 128:t * 128 + 384], True, True)
                scopy(p, sc, ps, scale=0.125)
                tt(p, sc, sc, ms[:, mi * 384:(mi + 1) * 384], "add")
                red(p, mx, sc, "max")
                ts(p, mx, mx, -1.0, "mult")
                act(p, sc, sc, AF.Exp, bias=mx)
                red(p, sm, sc, "add")
                vop(p, "reciprocal", sm, sm)
                ts(p, sc, sc, sm, "mult")
                for j in range(3):
                    trans(p, pt_ps, sc[:, j * 128:(j + 1) * 128], eyes)
                    copy(p, pt[:, j * 128:(j + 1) * 128], pt_ps)
                for j in range(3):
                    mm(p, ops_, pt[:, j * 128:(j + 1) * 128],
                       vs[:, (t + j) * 64:(t + j + 1) * 64], j == 0, j == 2)
                copy(p, ob, ops_)
                p.dma("gpsimd", o[u, t * 128:(t + 1) * 128, :], ob)
        return p
    return cached(("swa",), b)


def launch_swa(qT, kT, v):
    S = 4096
    W = 128
    masks = np.zeros((3, 128, 384), np.float32)
    qi = np.arange(128)[:, None]
    kj = np.arange(384)[None, :] - 128
    band = np.abs(qi - kj) <= W
    for mi, (lo_ok, hi_ok) in enumerate([(False, True), (True, True), (True, False)]):
        valid = band.copy()
        if not lo_ok:
            valid &= (kj >= 0)
        if not hi_ok:
            valid &= (kj < 128)
        masks[mi] = np.where(valid, 0.0, -1e30)
    ims = []
    for c in range(NCORE):
        qTu = np.zeros((8, 64, S), np.float32)
        kTu = np.zeros((8, 64, S + 256), np.float32)
        vu = np.zeros((8, S + 256, 64), np.float32)
        for u in range(8):
            g = c * 8 + u
            bb, hh = g // 16, g % 16
            qTu[u] = qT[hh * 64:(hh + 1) * 64, bb * S:(bb + 1) * S]
            kTu[u, :, 128:128 + S] = kT[hh * 64:(hh + 1) * 64, bb * S:(bb + 1) * S]
            vu[u, 128:128 + S] = v[bb * S:(bb + 1) * S, hh * 64:(hh + 1) * 64]
        ims.append({"qT": qTu, "kT": kTu, "vp": vu, "msk": masks, "eye": _EYE})
    rs = run(build_swa(), ims)
    o = np.zeros((NTOK, DM), np.float32)
    for c in range(NCORE):
        for u in range(8):
            g = c * 8 + u
            bb, hh = g // 16, g % 16
            o[bb * S:(bb + 1) * S, hh * 64:(hh + 1) * 64] = rs[c]["o"][u]
    return o


def build_combine():
    def b():
        p = Prog("combine")
        x = p.din("x", [TPC, DM])
        s = p.din("s", [TPC, DM])
        f = p.din("f", [TPC, DM])
        y = p.dout("y", [TPC, DM])
        xt = p.sbuf("xt", [128, DM])
        st = p.sbuf("st", [128, DM])
        ft = p.sbuf("ft", [128, DM])
        for t in range(TPC // 128):
            sl = slice(t * 128, (t + 1) * 128)
            wv = p.mark()
            p.dma("sync", xt, x[sl, :], wait=wv)
            p.dma("gpsimd", st, s[sl, :], wait=wv)
            p.dma("scalar", ft, f[sl, :], wait=wv)
            act(p, ft, ft, AF.Silu)
            tt(p, ft, ft, st, "mult")
            tt(p, ft, ft, xt, "add")
            p.dma("gpsimd", y[sl, :], ft)
        return p
    return cached(("combine",), b)


def launch_combine(x, s, f):
    p = build_combine()
    ims = [{"x": np.ascontiguousarray(x[c * TPC:(c + 1) * TPC]),
            "s": np.ascontiguousarray(s[c * TPC:(c + 1) * TPC]),
            "f": np.ascontiguousarray(f[c * TPC:(c + 1) * TPC])}
           for c in range(NCORE)]
    rs = run(p, ims)
    return np.concatenate([r["y"] for r in rs], 0)


def build_causal():
    S = 4096
    def b():
        p = Prog("causal")
        qT = p.din("qT", [8, 64, S])
        kT = p.din("kT", [8, 64, S])
        vv = p.din("v", [8, S, 64])
        cm = p.din("cmask", [128, 128])
        eye = p.din("eye", [128, 128])
        o = p.dout("o", [8, S, 64])
        qs = p.sbuf("qs", [64, S])
        ks = p.sbuf("ks", [64, S])
        vs = p.sbuf("vs", [128, 32 * 64])
        cms = p.sbuf("cms", [128, 128])
        eyes = p.sbuf("eyes", [128, 128])
        sc = p.sbuf("sc", [128, S])
        mx = p.sbuf("mx", [128, 1])
        sm = p.sbuf("sm", [128, 1])
        pt = p.sbuf("pt", [128, 128])
        ob = p.sbuf("ob", [128, 64])
        ps = p.psum("ps", [128, 512])
        ptp = p.psum("ptp", [128, 128])
        ops_ = p.psum("ops", [128, 64])
        p.dma("sync", cms, cm)
        p.dma("sync", eyes, eye)
        for u in range(8):
            wv = p.mark()
            p.dma("sync", qs, qT[u], wait=wv)
            p.dma("gpsimd", ks, kT[u], wait=wv)
            for j in range(32):
                p.dma("scalar", vs[:, j * 64:(j + 1) * 64],
                      vv[u, j * 128:(j + 1) * 128, :], wait=wv)
            for t in range(32):
                W = (t + 1) * 128
                for c in range((W + 511) // 512):
                    w = min(512, W - c * 512)
                    mm(p, ps[:, :w], qs[:, t * 128:(t + 1) * 128],
                       ks[:, c * 512:c * 512 + w], True, True)
                    scopy(p, sc[:, c * 512:c * 512 + w], ps[:, :w], scale=0.125)
                tt(p, sc[:, t * 128:W], sc[:, t * 128:W], cms, "add")
                red(p, mx, sc[:, :W], "max")
                ts(p, mx, mx, -1.0, "mult")
                act(p, sc[:, :W], sc[:, :W], AF.Exp, bias=mx)
                red(p, sm, sc[:, :W], "add")
                vop(p, "reciprocal", sm, sm)
                ts(p, sc[:, :W], sc[:, :W], sm, "mult")
                for j in range(t + 1):
                    trans(p, ptp, sc[:, j * 128:(j + 1) * 128], eyes)
                    copy(p, pt, ptp)
                    mm(p, ops_, pt, vs[:, j * 64:(j + 1) * 64], j == 0, j == t)
                copy(p, ob, ops_)
                p.dma("gpsimd", o[u, t * 128:(t + 1) * 128, :], ob)
        return p
    return cached(("causal",), b)


def launch_causal(qT, kT, v, kv_rep=1):
    S = 4096
    cmask = np.where(np.tril(np.ones((128, 128), bool)), 0.0, -1e30
                     ).astype(np.float32)
    ims = []
    for c in range(NCORE):
        qTu = np.zeros((8, 64, S), np.float32)
        kTu = np.zeros((8, 64, S), np.float32)
        vu = np.zeros((8, S, 64), np.float32)
        for u in range(8):
            g = c * 8 + u
            bb, hh = g // 16, g % 16
            kh = hh // kv_rep
            qTu[u] = qT[hh * 64:(hh + 1) * 64, bb * S:(bb + 1) * S]
            kTu[u] = kT[kh * 64:(kh + 1) * 64, bb * S:(bb + 1) * S]
            vu[u] = v[bb * S:(bb + 1) * S, kh * 64:(kh + 1) * 64]
        ims.append({"qT": qTu, "kT": kTu, "v": vu, "cmask": cmask, "eye": _EYE})
    rs = run(build_causal(), ims)
    o = np.zeros((NTOK, DM), np.float32)
    for c in range(NCORE):
        for u in range(8):
            g = c * 8 + u
            bb, hh = g // 16, g % 16
            o[bb * S:(bb + 1) * S, hh * 64:(hh + 1) * 64] = rs[c]["o"][u]
    return o


def build_linattn():
    S = 4096
    def b():
        p = Prog("linattn")
        qT = p.din("qT", [8, 64, S])
        kn = p.din("kn", [8, S, 64])
        vn = p.din("vn", [8, S, 65])
        o = p.dout("o", [8, S, 64])
        qs = p.sbuf("qs", [64, S])
        ks = p.sbuf("ks", [128, 32 * 64])
        vs = p.sbuf("vs", [128, 32 * 65])
        kvs = p.sbuf("kvs", [64, 65])
        zt = p.sbuf("zt", [128, 1])
        ob = p.sbuf("ob", [128, 64])
        pkv = p.psum("pkv", [64, 65])
        po = p.psum("po", [128, 65])
        for u in range(8):
            wv = p.mark()
            p.dma("sync", qs, qT[u], wait=wv)
            for j in range(32):
                p.dma("gpsimd", ks[:, j * 64:(j + 1) * 64],
                      kn[u, j * 128:(j + 1) * 128, :], wait=wv)
                p.dma("scalar", vs[:, j * 65:(j + 1) * 65],
                      vn[u, j * 128:(j + 1) * 128, :], wait=wv)
            for j in range(32):
                mm(p, pkv, ks[:, j * 64:(j + 1) * 64], vs[:, j * 65:(j + 1) * 65],
                   j == 0, j == 31)
            copy(p, kvs, pkv)
            for t in range(32):
                mm(p, po, qs[:, t * 128:(t + 1) * 128], kvs, True, True)
                ts(p, zt, po[:, 64:65], 1e-6, "add")
                vop(p, "reciprocal", zt, zt)
                ts(p, ob, po[:, 0:64], zt, "mult")
                p.dma("gpsimd", o[u, t * 128:(t + 1) * 128, :], ob)
        return p
    return cached(("linattn",), b)


def launch_linattn(qT, k, v):
    S = 4096
    ims = []
    for c in range(NCORE):
        qTu = np.zeros((8, 64, S), np.float32)
        knu = np.zeros((8, S, 64), np.float32)
        vnu = np.zeros((8, S, 65), np.float32)
        vnu[:, :, 64] = 1.0
        for u in range(8):
            g = c * 8 + u
            bb, hh = g // 16, g % 16
            qTu[u] = qT[hh * 64:(hh + 1) * 64, bb * S:(bb + 1) * S]
            knu[u] = k[bb * S:(bb + 1) * S, hh * 64:(hh + 1) * 64]
            vnu[u, :, :64] = v[bb * S:(bb + 1) * S, hh * 64:(hh + 1) * 64]
        ims.append({"qT": qTu, "kn": knu, "vn": vnu})
    rs = run(build_linattn(), ims)
    o = np.zeros((NTOK, DM), np.float32)
    for c in range(NCORE):
        for u in range(8):
            g = c * 8 + u
            bb, hh = g // 16, g % 16
            o[bb * S:(bb + 1) * S, hh * 64:(hh + 1) * 64] = rs[c]["o"][u]
    return o


def build_router(mode, E=8):
    def b():
        p = Prog(f"router{mode}")
        xT = p.din("xT", [DM, TPC])
        rw = p.din("rw", [DM, E])
        br = p.din("br", [128, E]) if mode == "softmax" else None
        c = p.dout("c", [TPC, E])
        xb = p.sbuf("xb", [128, 128])
        rws = p.sbuf("rws", [128, 8 * E])
        brs = p.sbuf("brs", [128, E]) if mode == "softmax" else None
        lg = p.sbuf("lg", [128, E])
        m1 = p.sbuf("m1", [128, 1])
        m2 = p.sbuf("m2", [128, 1])
        i1 = p.sbuf("i1", [128, E])
        i2 = p.sbuf("i2", [128, E])
        w1 = p.sbuf("w1", [128, 1])
        w2 = p.sbuf("w2", [128, 1])
        pm = p.sbuf("pm", [128, E])
        ps = p.psum("ps", [128, E])
        wv = p.mark()
        for kc in range(8):
            p.dma("sync", rws[:, kc * E:(kc + 1) * E],
                  rw[kc * 128:(kc + 1) * 128, :], wait=wv)
        if mode == "softmax":
            p.dma("gpsimd", brs, br, wait=wv)
        for t in range(TPC // 128):
            wv = p.mark()
            for kc in range(8):
                p.dma("sync", xb, xT[kc * 128:(kc + 1) * 128,
                                     t * 128:(t + 1) * 128], wait=None)
                mm(p, ps, xb, rws[:, kc * E:(kc + 1) * E], kc == 0, kc == 7)
            if mode == "softmax":
                tt(p, lg, ps, brs, "add")
                red(p, m1, lg, "max")
                ts(p, m1, m1, -1.0, "mult")
                act(p, lg, lg, AF.Exp, bias=m1)
                red(p, m2, lg, "add")
                vop(p, "reciprocal", m2, m2)
                ts(p, lg, lg, m2, "mult")
            elif mode == "top2prob":
                red(p, m1, ps, "max")
                ts(p, m1, m1, -1.0, "mult")
                act(p, lg, ps, AF.Exp, bias=m1)
                red(p, m2, lg, "add")
                vop(p, "reciprocal", m2, m2)
                ts(p, lg, lg, m2, "mult")
                red(p, m1, lg, "max")
                ts(p, i1, lg, m1, "is_ge")
                p.op("vector", lambda e, o=pm, a=i1, s=-1e30, bb_=lg:
                     e.scalar_tensor_tensor(out=o, in0=a, scalar=s, in1=bb_,
                                            op0=ALU.mult, op1=ALU.add))
                red(p, m2, pm, "max")
                ts(p, i2, lg, m2, "is_ge")
                tt(p, lg, lg, i2, "mult")
            else:
                copy(p, lg, ps)
                red(p, m1, lg, "max")
                ts(p, i1, lg, m1, "is_ge")
                p.op("vector", lambda e, o=pm, a=i1, s=-1e30, bb_=lg:
                     e.scalar_tensor_tensor(out=o, in0=a, scalar=s, in1=bb_,
                                            op0=ALU.mult, op1=ALU.add))
                red(p, m2, pm, "max")
                ts(p, i2, lg, m2, "is_ge")
                tt(p, i2, i2, i1, "subtract")
                tt(p, w1, m1, m2, "subtract")
                act(p, w1, w1, AF.Sigmoid)
                ts(p, w2, w1, -1.0, "mult", 1.0, "add")
                ts(p, i1, i1, w1, "mult")
                ts(p, i2, i2, w2, "mult")
                tt(p, lg, i1, i2, "add")
            p.dma("gpsimd", c[t * 128:(t + 1) * 128, :], lg)
        return p
    return cached(("router", mode, E), b)


def launch_router(mode, xT, rw, br=None):
    E = rw.shape[1]
    p = build_router(mode, E)
    ims = []
    for c in range(NCORE):
        im = {"xT": np.ascontiguousarray(xT[:, c * TPC:(c + 1) * TPC]),
              "rw": np.ascontiguousarray(rw)}
        if mode == "softmax":
            im["br"] = np.ascontiguousarray(
                np.broadcast_to(np.asarray(br)[None, :], (128, E)))
        ims.append(im)
    rs = run(p, ims)
    return np.concatenate([r["c"] for r in rs], 0)


def build_bisect(kk=2048, iters=45):
    def b():
        p = Prog("bisect")
        sm = p.din("sm", [128, 128])
        ones = p.din("ones", [128, 128])
        c = p.dout("c", [128, 128])
        sms = p.sbuf("sms", [128, 128])
        ons = p.sbuf("ons", [128, 128])
        lo = p.sbuf("lo", [1, 1])
        hi = p.sbuf("hi", [1, 1])
        mid = p.sbuf("mid", [1, 1])
        tq = p.sbuf("tq", [1, 1])
        cond = p.sbuf("cond", [1, 1])
        cond2 = p.sbuf("cond2", [1, 1])
        cnt = p.sbuf("cnt", [1, 1])
        mrep = p.sbuf("mrep", [128, 1])
        ind = p.sbuf("ind", [128, 128])
        cs = p.sbuf("cs", [128, 1])
        pr = p.psum("pr", [128, 1])
        pc = p.psum("pc", [1, 1])
        wv = p.mark()
        p.dma("sync", sms, sm, wait=wv)
        p.dma("gpsimd", ons, ones, wait=wv)
        memset(p, lo, 0.0)
        memset(p, hi, 1.0)
        for it in range(iters):
            tt(p, mid, lo, hi, "add")
            ts(p, mid, mid, 0.5, "mult")
            mm(p, pr, ons[0:1, :], mid, True, True)
            copy(p, mrep, pr)
            ts(p, ind, sms, mrep, "is_ge")
            red(p, cs, ind, "add")
            mm(p, pc, cs, ons[:, 0:1], True, True)
            copy(p, cnt, pc)
            ts(p, cond, cnt, float(kk), "is_ge")
            ts(p, cond2, cond, -1.0, "mult", 1.0, "add")
            tt(p, tq, mid, lo, "subtract")
            tt(p, tq, tq, cond, "mult")
            tt(p, lo, lo, tq, "add")
            tt(p, tq, mid, hi, "subtract")
            tt(p, tq, tq, cond2, "mult")
            tt(p, hi, hi, tq, "add")
        mm(p, pr, ons[0:1, :], lo, True, True)
        copy(p, mrep, pr)
        ts(p, ind, sms, mrep, "is_ge")
        tt(p, ind, ind, sms, "mult")
        p.dma("gpsimd", c, ind)
        return p
    return cached(("bisect", kk, iters), b)


def launch_bisect(Sm):
    ones = np.ones((128, 128), np.float32)
    ims = [{"sm": np.ascontiguousarray(Sm[:, e].reshape(128, 128)),
            "ones": ones} for e in range(8)]
    rs = run(build_bisect(), ims)
    c = np.zeros((NTOK, 8), np.float32)
    for e in range(8):
        c[:, e] = rs[e]["c"].reshape(-1)
    return c


def build_conv(C, l2heads):
    def b():
        p = Prog(f"conv{C}")
        xp = p.din("xp", [TPC + 3, C])
        wr = p.din("wr", [128, 4 * C])
        y = p.dout("y", [TPC, C])
        xb = [p.sbuf(f"xb{j}", [128, C]) for j in range(4)]
        wrs = p.sbuf("wrs", [128, 4 * C])
        acc = p.sbuf("acc", [128, C])
        tmp = p.sbuf("tmp", [128, C])
        sq = p.sbuf("sq", [128, C])
        nh = C // 64
        ss = p.sbuf("ss", [128, max(nh, 1)])
        p.dma("sync", wrs, wr)
        for t in range(TPC // 128):
            wv = p.mark()
            for j in range(4):
                p.dma(["sync", "gpsimd", "scalar"][j % 3], xb[j],
                      xp[t * 128 + j:t * 128 + j + 128, :], wait=wv)
            tt(p, acc, xb[0], wrs[:, 0:C], "mult")
            for j in range(1, 4):
                tt(p, tmp, xb[j], wrs[:, j * C:(j + 1) * C], "mult")
                tt(p, acc, acc, tmp, "add")
            act(p, acc, acc, AF.Silu)
            if l2heads:
                act(p, sq, acc, AF.Square)
                for h in range(nh):
                    red(p, ss[:, h:h + 1], sq[:, h * 64:(h + 1) * 64], "add")
                act(p, ss, ss, AF.Sqrt)
                ts(p, ss, ss, 1e-12, "max")
                vop(p, "reciprocal", ss, ss)
                for h in range(nh):
                    ts(p, acc[:, h * 64:(h + 1) * 64],
                       acc[:, h * 64:(h + 1) * 64], ss[:, h:h + 1], "mult")
            p.dma("gpsimd", y[t * 128:(t + 1) * 128, :], acc)
        return p
    return cached(("conv", C, l2heads), b)


def launch_conv(x_nat, w, l2heads):
    C = x_nat.shape[1]
    p = build_conv(C, l2heads)
    wrep = np.zeros((128, 4 * C), np.float32)
    for j in range(4):
        wrep[:, j * C:(j + 1) * C] = np.asarray(w)[:, j][None, :]
    ims = []
    for c in range(NCORE):
        start = c * TPC
        xp = np.zeros((TPC + 3, C), np.float32)
        if start % 4096 == 0:
            xp[3:] = x_nat[start:start + TPC]
        else:
            xp[:] = x_nat[start - 3:start + TPC]
        ims.append({"xp": xp, "wr": wrep})
    rs = run(p, ims)
    return np.concatenate([r["y"] for r in rs], 0)


def build_scanprep():
    NH = 12
    def b():
        p = Prog("scanprep")
        gr = p.din("gr", [TPC, NH])
        br = p.din("br", [TPC, NH])
        al = p.din("al", [1, NH])
        ones = p.din("ones", [1, 128])
        gk = p.dout("gk", [TPC, NH])
        be = p.dout("be", [TPC, NH])
        als = p.sbuf("als", [1, NH])
        ons = p.sbuf("ons", [1, 128])
        narep = p.sbuf("narep", [128, NH])
        gb = p.sbuf("gb", [128, NH])
        bb_ = p.sbuf("bb", [128, NH])
        r0 = p.sbuf("r0", [128, NH])
        ab = p.sbuf("ab", [128, NH])
        pn = p.psum("pn", [128, NH])
        wv = p.mark()
        p.dma("sync", als, al, wait=wv)
        p.dma("gpsimd", ons, ones, wait=wv)
        act(p, als, als, AF.Exp)
        ts(p, als, als, -1.0, "mult")
        mm(p, pn, ons, als, True, True)
        copy(p, narep, pn)
        for t in range(TPC // 128):
            sl = slice(t * 128, (t + 1) * 128)
            wv = p.mark()
            p.dma("sync", gb, gr[sl, :], wait=wv)
            p.dma("gpsimd", bb_, br[sl, :], wait=wv)
            ts(p, r0, gb, 0.0, "max")
            act(p, ab, gb, AF.Abs)
            act(p, ab, ab, AF.Exp, scale=-1.0)
            act(p, ab, ab, AF.Ln, bias=1.0)
            tt(p, r0, r0, ab, "add")
            tt(p, r0, r0, narep, "mult")
            p.dma("gpsimd", gk[sl, :], r0)
            act(p, bb_, bb_, AF.Sigmoid)
            p.dma("scalar", be[sl, :], bb_)
        return p
    return cached(("scanprep",), b)


def launch_scanprep(graw, braw, A_log):
    p = build_scanprep()
    ones = np.ones((1, 128), np.float32)
    al = np.asarray(A_log, np.float32).reshape(1, 12)
    ims = [{"gr": np.ascontiguousarray(graw[c * TPC:(c + 1) * TPC]),
            "br": np.ascontiguousarray(braw[c * TPC:(c + 1) * TPC]),
            "al": al, "ones": ones} for c in range(NCORE)]
    rs = run(p, ims)
    gk = np.concatenate([r["gk"] for r in rs], 0)
    be = np.concatenate([r["be"] for r in rs], 0)
    return gk, be


def build_scan():
    S, CH, DK, DV = 4096, 128, 64, 128
    NCHUNK = S // CH
    def b():
        p = Prog("scan")
        qT = p.din("qT", [6, DK, S])
        kT = p.din("kT", [6, DK, S])
        kn = p.din("kn", [6, S, DK])
        vn = p.din("vn", [6, S, DV])
        g6 = p.din("g6", [6, S])
        b6 = p.din("b6", [6, S])
        eye = p.din("eye", [128, 128])
        ltt = p.din("ltt", [128, 128])
        sln = p.din("sln", [128, 128])
        il = p.din("il", [128, 128])
        ones = p.din("ones", [128, 128])
        o6 = p.dout("o6", [6, S, DV])

        eyes = p.sbuf("eyes", [128, 128])
        ltts = p.sbuf("ltts", [128, 128])
        slns = p.sbuf("slns", [128, 128])
        ils = p.sbuf("ils", [128, 128])
        ons = p.sbuf("ons", [128, 128])
        S_ = p.sbuf("S_", [DK, DV])
        qt = p.sbuf("qt", [DK, CH])
        kt = p.sbuf("kt", [DK, CH])
        knt = p.sbuf("knt", [CH, DK])
        vt = p.sbuf("vt", [CH, DV])
        gcol = p.sbuf("gcol", [CH, 1])
        bcol = p.sbuf("bcol", [CH, 1])
        G = p.sbuf("G", [CH, 1])
        Acol = p.sbuf("Acol", [CH, 1])
        GC = p.sbuf("GC", [1, 1])
        GCr = p.sbuf("GCr", [CH, 1])
        wcol = p.sbuf("wcol", [CH, 1])
        Grow = p.sbuf("Grow", [1, CH])
        expo = p.sbuf("expo", [CH, CH])
        Dm = p.sbuf("Dm", [CH, CH])
        Nm = p.sbuf("Nm", [CH, CH])
        Tm = p.sbuf("Tm", [CH, CH])
        Pm = p.sbuf("Pm", [CH, CH])
        PTm = p.sbuf("PTm", [CH, CH])
        RHS = p.sbuf("RHS", [CH, DV])
        Um = p.sbuf("Um", [CH, DV])
        PPm = p.sbuf("PPm", [CH, CH])
        Ob = p.sbuf("Ob", [CH, DV])
        Kd = p.sbuf("Kd", [CH, DK])
        bA = p.sbuf("bA", [CH, 1])
        bV = p.sbuf("bV", [CH, DV])
        GC64 = p.sbuf("GC64", [DK, 1])
        pA = p.psum("pA", [128, 128])
        pB = p.psum("pB", [128, 128])
        pS = p.psum("pS", [DK, DV])
        pV = p.psum("pV", [128, 1])
        pW = p.psum("pW", [DK, 1])

        wv = p.mark()
        p.dma("sync", eyes, eye, wait=wv)
        p.dma("sync", ltts, ltt, wait=wv)
        p.dma("gpsimd", slns, sln, wait=wv)
        p.dma("gpsimd", ils, il, wait=wv)
        p.dma("scalar", ons, ones, wait=wv)
        for u in range(6):
            memset(p, S_, 0.0)
            for c in range(NCHUNK):
                sl = slice(c * CH, (c + 1) * CH)
                wv = p.mark()
                p.dma("sync", qt, qT[u, :, sl], wait=wv)
                p.dma("sync", kt, kT[u, :, sl], wait=wv)
                p.dma("gpsimd", knt, kn[u, sl, :], wait=wv)
                p.dma("gpsimd", vt, vn[u, sl, :], wait=wv)
                p.dma("scalar", gcol, g6[u, sl][:, None], wait=wv)
                p.dma("scalar", bcol, b6[u, sl][:, None], wait=wv)
                mm(p, pV, ltts, gcol, True, True)
                copy(p, G, pV)
                act(p, Acol, G, AF.Exp)
                mm(p, pW[0:1, :], gcol, ons[:, 0:1], True, True)
                copy(p, GC, pW[0:1, :])
                mm(p, pV, ons[0:1, :], GC, True, True)
                copy(p, GCr, pV)
                act(p, wcol, G, AF.Exp, scale=-1.0, bias=GCr)
                mm(p, pA[0:1, :], G, eyes, True, True)
                copy(p, Grow, pA[0:1, :])
                mm(p, pA, ons[0:1, :], Grow, True, True)
                ts(p, expo, pA, G, "subtract", 0.0, "max")
                act(p, Dm, expo, AF.Exp, scale=-1.0)
                mm(p, pA, kt, kt, True, True)
                tt(p, Nm, Dm, pA, "mult")
                ts(p, Nm, Nm, bcol, "mult")
                tt(p, Nm, Nm, slns, "mult")
                tt(p, Tm, Nm, eyes, "add")
                copy(p, Pm, Nm)
                for lev in range(6):
                    trans(p, pB, Pm, eyes)
                    copy(p, PTm, pB)
                    mm(p, pA, PTm, Pm, True, True)
                    copy(p, Pm, pA)
                    trans(p, pB, Tm, eyes)
                    copy(p, PTm, pB)
                    mm(p, pA, PTm, Pm, True, True)
                    tt(p, Tm, Tm, pA, "add")
                mm(p, pA[:, 0:DV], kt, S_, True, True)
                tt(p, bA, bcol, Acol, "mult")
                ts(p, bA, bA, -1.0, "mult")
                ts(p, bV, vt, bcol, "mult")
                p.op("vector", lambda e, o=RHS, i0=pA[:, 0:DV], s=bA, i1=bV:
                     e.scalar_tensor_tensor(out=o, in0=i0, scalar=s, in1=i1,
                                            op0=ALU.mult, op1=ALU.add))
                trans(p, pB, Tm, eyes)
                copy(p, PTm, pB)
                mm(p, pA[:, 0:DV], PTm, RHS, True, True)
                copy(p, Um, pA[:, 0:DV])
                mm(p, pA, qt, kt, True, True)
                tt(p, PPm, Dm, pA, "mult")
                tt(p, PPm, PPm, ils, "mult")
                mm(p, pA[:, 0:DV], qt, S_, True, True)
                ts(p, Ob, pA[:, 0:DV], Acol, "mult")
                trans(p, pB, PPm, eyes)
                copy(p, PTm, pB)
                mm(p, pA[:, 0:DV], PTm, Um, True, True)
                tt(p, Ob, Ob, pA[:, 0:DV], "add")
                p.dma("gpsimd", o6[u, sl, :], Ob)
                ts(p, Kd, knt, wcol, "mult")
                mm(p, pS, Kd, Um, True, True)
                mm(p, pW, ons[0:1, 0:DK], GC, True, True)
                copy(p, GC64, pW)
                act(p, GC64, GC64, AF.Exp)
                ts(p, S_, S_, GC64, "mult")
                tt(p, S_, S_, pS, "add")
        return p
    return cached(("scan",), b)


def launch_scan(qT, kT, k, v, gk, beta):
    S = 4096
    ltt = np.ascontiguousarray(np.tril(np.ones((128, 128), np.float32)).T)
    sln = np.where(np.tril(np.ones((128, 128), bool), -1), -1.0, 0.0
                   ).astype(np.float32)
    il = np.tril(np.ones((128, 128), np.float32))
    ones = np.ones((128, 128), np.float32)
    ims = []
    for c in range(NCORE):
        qTu = np.zeros((6, 64, S), np.float32)
        kTu = np.zeros((6, 64, S), np.float32)
        knu = np.zeros((6, S, 64), np.float32)
        vnu = np.zeros((6, S, 128), np.float32)
        g6 = np.zeros((6, S), np.float32)
        b6 = np.zeros((6, S), np.float32)
        for u in range(6):
            g = c * 6 + u
            bb, hh = g // 12, g % 12
            qTu[u] = qT[hh * 64:(hh + 1) * 64, bb * S:(bb + 1) * S]
            kTu[u] = kT[hh * 64:(hh + 1) * 64, bb * S:(bb + 1) * S]
            knu[u] = k[bb * S:(bb + 1) * S, hh * 64:(hh + 1) * 64]
            vnu[u] = v[bb * S:(bb + 1) * S, hh * 128:(hh + 1) * 128]
            g6[u] = gk[bb * S:(bb + 1) * S, hh]
            b6[u] = beta[bb * S:(bb + 1) * S, hh]
        ims.append({"qT": qTu, "kT": kTu, "kn": knu, "vn": vnu,
                    "g6": g6, "b6": b6, "eye": _EYE, "ltt": ltt,
                    "sln": sln, "il": il, "ones": ones})
    rs = run(build_scan(), ims)
    o = np.zeros((NTOK, 1536), np.float32)
    for c in range(NCORE):
        for u in range(6):
            g = c * 6 + u
            bb, hh = g // 12, g % 12
            o[bb * S:(bb + 1) * S, hh * 128:(hh + 1) * 128] = rs[c]["o6"][u]
    return o


def build_opost():
    C = 1536
    def b():
        p = Prog("opost")
        o = p.din("o", [TPC, C])
        gp = p.din("gp", [TPC, C])
        gw = p.din("gw", [128, C])
        y = p.dout("y", [TPC, C])
        ot = p.sbuf("ot", [128, C])
        gt = p.sbuf("gt", [128, C])
        gws = p.sbuf("gws", [128, C])
        sq = p.sbuf("sq", [128, C])
        ss = p.sbuf("ss", [128, 12])
        p.dma("sync", gws, gw)
        for t in range(TPC // 128):
            sl = slice(t * 128, (t + 1) * 128)
            wv = p.mark()
            p.dma("sync", ot, o[sl, :], wait=wv)
            p.dma("gpsimd", gt, gp[sl, :], wait=wv)
            act(p, sq, ot, AF.Square)
            for h in range(12):
                red(p, ss[:, h:h + 1], sq[:, h * 128:(h + 1) * 128], "add")
            ts(p, ss, ss, 1.0 / 128, "mult")
            act(p, ss, ss, AF.Sqrt, bias=1e-5)
            vop(p, "reciprocal", ss, ss)
            for h in range(12):
                ts(p, ot[:, h * 128:(h + 1) * 128],
                   ot[:, h * 128:(h + 1) * 128], ss[:, h:h + 1], "mult")
            tt(p, ot, ot, gws, "mult")
            act(p, gt, gt, AF.Silu)
            tt(p, ot, ot, gt, "mult")
            p.dma("gpsimd", y[sl, :], ot)
        return p
    return cached(("opost",), b)


def launch_opost(o, gproj, gnorm_w):
    p = build_opost()
    gw = np.zeros((128, 1536), np.float32)
    gw[:] = np.tile(np.asarray(gnorm_w, np.float32), 12)[None, :]
    ims = [{"o": np.ascontiguousarray(o[c * TPC:(c + 1) * TPC]),
            "gp": np.ascontiguousarray(gproj[c * TPC:(c + 1) * TPC]),
            "gw": gw} for c in range(NCORE)]
    rs = run(p, ims)
    return np.concatenate([r["y"] for r in rs], 0)


def build_ew(variant, E=8):
    def b():
        p = Prog(f"ew{variant}")
        y = p.dout("y", [TPC, DM])
        at = p.sbuf("at", [128, DM])
        bt = p.sbuf("bt", [128, DM])
        acc = p.sbuf("acc", [128, DM])
        if variant == "add2":
            a = p.din("a", [TPC, DM])
            bb_ = p.din("b", [TPC, DM])
            for t in range(TPC // 128):
                sl = slice(t * 128, (t + 1) * 128)
                wv = p.mark()
                p.dma("sync", at, a[sl, :], wait=wv)
                p.dma("gpsimd", bt, bb_[sl, :], wait=wv)
                tt(p, acc, at, bt, "add")
                p.dma("gpsimd", y[sl, :], acc)
            return p
        o8 = p.din("o8", [E, TPC, DM])
        cc = p.din("c", [TPC, E])
        ct = p.sbuf("ct", [128, E])
        ot = [p.sbuf(f"ot{e}", [128, DM]) for e in range(E)]
        if variant == "moesilu":
            x = p.din("x", [TPC, DM])
            s = p.din("s", [TPC, DM])
        else:
            x = p.din("base", [TPC, DM])
        for t in range(TPC // 128):
            sl = slice(t * 128, (t + 1) * 128)
            wv = p.mark()
            qs = ["sync", "gpsimd", "scalar"]
            for e in range(E):
                p.dma(qs[e % 3], ot[e], o8[e, sl.start:sl.stop, :], wait=wv)
            p.dma("sync", ct, cc[sl, :], wait=wv)
            p.dma("gpsimd", at, x[sl, :], wait=wv)
            if variant == "moesilu":
                p.dma("scalar", bt, s[sl, :], wait=wv)
            ts(p, acc, ot[0], ct[:, 0:1], "mult")
            for e in range(1, E):
                p.op("vector", lambda ee, o=acc, i0=ot[e], sc=ct[:, e:e + 1],
                     i1=acc: ee.scalar_tensor_tensor(
                         out=o, in0=i0, scalar=sc, in1=i1,
                         op0=ALU.mult, op1=ALU.add))
            if variant == "moesilu":
                act(p, acc, acc, AF.Silu)
                tt(p, acc, acc, bt, "mult")
            tt(p, acc, acc, at, "add")
            p.dma("gpsimd", y[sl, :], acc)
        return p
    return cached(("ew", variant, E), b)


def launch_ew(variant, o8=None, c=None, x=None, s=None, a=None, b=None):
    p = build_ew(variant)
    ims = []
    for cix in range(NCORE):
        sl = slice(cix * TPC, (cix + 1) * TPC)
        if variant == "add2":
            ims.append({"a": np.ascontiguousarray(a[sl]),
                        "b": np.ascontiguousarray(b[sl])})
        else:
            im = {"o8": np.ascontiguousarray(np.stack(
                      [oe[sl] for oe in o8], 0)),
                  "c": np.ascontiguousarray(c[sl])}
            if variant == "moesilu":
                im["x"] = np.ascontiguousarray(x[sl])
                im["s"] = np.ascontiguousarray(s[sl])
            else:
                im["base"] = np.ascontiguousarray(x[sl])
            ims.append(im)
    rs = run(p, ims)
    return np.concatenate([r["y"] for r in rs], 0)


def block0(x, inp):
    xn = launch_norm("ln", x)
    xnT = T(xn)
    wqkvT = np.concatenate([T(inp["L0_attn_Wq"]), T(inp["L0_attn_Wk"]),
                            T(inp["L0_attn_Wv"])], axis=1)
    qkvT = launch_gemm(xnT, wqkvT)
    o = launch_swa(qkvT[0:1024], qkvT[1024:2048], T(qkvT[2048:3072]))
    sT = launch_gemm(T(o), T(inp["L0_attn_Wo"]))
    hT = launch_gemm(xnT, T(inp["L0_ffn_w1"]), "gelu",
                     bias=inp["L0_ffn_b1"], wnat=inp["L0_ffn_w1"])
    fT = launch_gemm(hT, T(inp["L0_ffn_w2"]), "none",
                     bias=inp["L0_ffn_b2"], wnat=inp["L0_ffn_w2"])
    return launch_combine(x, T(sT), T(fT))


def block1(x, inp):
    xn = launch_norm("ln", x)
    xnT = T(xn)
    qkvT = launch_gemm(xnT, T(inp["L1_attn_w_qkv"]), "elu1", elu1_blocks=16)
    o = launch_linattn(qkvT[0:1024], T(qkvT[1024:2048]), T(qkvT[2048:3072]))
    sT = launch_gemm(T(o), T(inp["L1_attn_w_out"]), bias=inp["L1_attn_b_out"])
    c = launch_router("top2prob", xnT, T(inp["L1_ffn_router_w"]))
    o8 = []
    for e in range(8):
        hT = launch_gemm(xnT, np.asarray(inp["L1_ffn_w1"][e]), "relu")
        oT = launch_gemm(hT, np.asarray(inp["L1_ffn_w2"][e]))
        o8.append(T(oT))
    return launch_ew("moesilu", o8=o8, c=c, x=x, s=T(sT))


def block2(x, inp):
    xn = launch_norm("ln", x)
    xnT = T(xn)
    s2 = launch_causal(xnT, xnT, xn)
    Sm = launch_router("softmax", xnT, np.asarray(inp["L2_ffn_Wr"]),
                       br=inp["L2_ffn_br"])
    c = launch_bisect(Sm)
    midT = {}
    for g in range(4):
        midT[g] = launch_gemm(xnT, np.asarray(inp["L2_ffn_W_up"][g]), "silu",
                              bias=inp["L2_ffn_b_up"][g])
    o8 = []
    for e in range(8):
        oT = launch_gemm(midT[e // 2], np.asarray(inp["L2_ffn_W_down"][e]),
                         bias=inp["L2_ffn_b_down"][e])
        o8.append(T(oT))
    return launch_ew("moesilu", o8=o8, c=c, x=x, s=s2)


def block3(x, inp):
    xn = launch_norm("ln", x)
    xnT = T(xn)
    WT = np.concatenate([np.asarray(inp["L3_attn_Wq"]),
                         np.asarray(inp["L3_attn_Wk"]),
                         np.asarray(inp["L3_attn_Wv"]),
                         np.asarray(inp["L3_attn_Wg"]),
                         np.asarray(inp["L3_attn_Wgk"]),
                         np.asarray(inp["L3_attn_Wb"])], axis=1)
    bias = np.zeros(4632, np.float32)
    bias[4608:4620] = np.asarray(inp["L3_attn_dt_bias"])
    bias[4620:4632] = np.asarray(inp["L3_attn_bb"])
    projT = launch_gemm(xnT, WT, bias=bias)
    q = launch_conv(T(projT[0:768]), inp["L3_attn_conv_q"], True)
    k = launch_conv(T(projT[768:1536]), inp["L3_attn_conv_k"], True)
    v = launch_conv(T(projT[1536:3072]), inp["L3_attn_conv_v"], False)
    gproj = T(projT[3072:4608])
    gk, beta = launch_scanprep(T(projT[4608:4620]), T(projT[4620:4632]),
                               inp["L3_attn_A_log"])
    o = launch_scan(T(q), T(k), k, v, gk, beta)
    og = launch_opost(o, gproj, inp["L3_attn_gnorm_w"])
    s3T = launch_gemm(T(og), np.asarray(inp["L3_attn_Wo"]))
    a = launch_norm("rms", xn, w=inp["L3_ffn_attn_norm_w"], eps=1e-6)
    aT = T(a)
    qkvT = launch_gemm(aT, np.concatenate(
        [T(inp["L3_ffn_wq"]), T(inp["L3_ffn_wk"]), T(inp["L3_ffn_wv"])], 1))
    attn = launch_causal(qkvT[0:1024], qkvT[1024:1280],
                         T(qkvT[1280:1536]), kv_rep=4)
    awT = launch_gemm(T(attn), T(inp["L3_ffn_wo"]))
    h2 = launch_ew("add2", a=xn, b=T(awT))
    h2n = launch_norm("rms", h2, w=inp["L3_ffn_ffn_norm_w"], eps=1e-6)
    h2nT = T(h2n)
    cmb = launch_router("top2renorm", h2nT, T(inp["L3_ffn_gate_w"]))
    o8 = []
    for e in range(8):
        oT = launch_gemm(h2nT, T(np.asarray(inp["L3_ffn_expert_w"][e])))
        o8.append(T(oT))
    f3 = launch_ew("moeadd", o8=o8, c=cmb, x=h2)
    return launch_combine(x, T(s3T), f3)


def kernel(**inp):
    x = np.ascontiguousarray(np.asarray(inp["L0_attn_x"], np.float32)
                             .reshape(NTOK, DM))
    x = block0(x, inp)
    x = block1(x, inp)
    x = block2(x, inp)
    x = block3(x, inp)
    return x.reshape(4, 4096, DM)


# version 02 (final)
# speedup vs baseline: 1.0353x; 1.0353x over previous
import contextlib
import numpy as np
import concourse.bass as bass
import concourse.mybir as mybir
from concourse.bass_utils import run_bass_kernel_spmd

F32 = mybir.dt.float32
AF = mybir.ActivationFunctionType
ALU = mybir.AluOpType
AX = mybir.AxisListType

NCORE = 8
NTOK = 16384
TPC = NTOK // NCORE
DM = 1024

_EYE = np.eye(128, dtype=np.float32)
_ONES_ROW = np.ones((1, 128), dtype=np.float32)


def T(a):
    return np.ascontiguousarray(np.asarray(a).T)


class Prog:
    def __init__(self, name):
        self.name = name
        self.nc = bass.Bass("TRN2", target_bir_lowering=False)
        self.ops = []
        self._cum = 0
        self.finished = False
        self.consts = {}

    def cf(self, v):
        v = float(v)
        if v not in self.consts:
            self.consts[v] = len(self.consts)
        return v

    def din(self, name, shape):
        return self.nc.dram_tensor(name, list(shape), F32, kind="ExternalInput").ap()

    def dout(self, name, shape):
        return self.nc.dram_tensor(name, list(shape), F32, kind="ExternalOutput").ap()

    def sbuf(self, name, shape):
        return self.nc.alloc_sbuf_tensor(name, list(shape), F32).ap()

    def psum(self, name, shape):
        return self.nc.alloc_psum_tensor(name, list(shape), F32).ap()

    def op(self, eng, fn, inc=1, wait=None):
        self.ops.append((eng, fn, inc, wait))
        self._cum += inc

    def mark(self):
        return self._cum

    def dma(self, eng, dst, src, wait=None):
        self.op(eng, lambda e, d=dst, s=src: e.dma_start(out=d, in_=s), 16, wait)

    def finish(self):
        assert not self.finished
        self.finished = True
        nc = self.nc
        ncst = len(self.consts)
        if ncst:
            cpool = nc.alloc_sbuf_tensor("cpool_", [128, ncst], F32).ap()
            cin = nc.dram_tensor("consts_", [128, ncst], F32,
                                 kind="ExternalInput").ap()
            for v, i in self.consts.items():
                nc.const_aps.register(F32, v, cpool[:, i:i + 1])
            self.ops.insert(
                0, ("sync", lambda e, d=cpool, s=cin: e.dma_start(out=d, in_=s),
                    16, None))
            arr = np.zeros((128, ncst), np.float32)
            for v, i in self.consts.items():
                arr[:, i] = v
            self.const_arr = arr
        else:
            self.const_arr = None
        shift = 16 if ncst else 0
        sched = []
        cum = 0
        for (eng, fn, inc, wait) in self.ops:
            w = cum if wait is None else wait + shift
            sched.append((eng, fn, inc, w))
            cum += inc
        total = cum
        used = {e for (e, _, _, _) in sched} | {"vector"}
        self._stack = contextlib.ExitStack()
        sem = self._stack.enter_context(nc.semaphore("tick"))
        blk = self._stack.enter_context(nc.Block())

        def mk(name):
            def emit(e):
                for (eng, fn, inc, w) in sched:
                    if eng != name:
                        continue
                    if w > 0:
                        e.wait_ge(sem, w)
                    inst = fn(e)
                    inst.then_inc(sem, inc)
                if name == "vector":
                    e.wait_ge(sem, total)
            return emit

        for ename in ("sync", "tensor", "scalar", "vector", "gpsimd"):
            if ename in used:
                getattr(blk, ename)(mk(ename))
        self._stack.close()
        return nc


_PROG_CACHE = {}


def cached(key, builder):
    if key not in _PROG_CACHE:
        p = builder()
        p.finish()
        _PROG_CACHE[key] = p
    return _PROG_CACHE[key]


def run(prog, in_maps):
    if prog.const_arr is not None:
        for im in in_maps:
            im["consts_"] = prog.const_arr
    res = run_bass_kernel_spmd(prog.nc, in_maps, core_ids=list(range(NCORE)))
    return res.results


def act(p, out, in_, func, bias=0.0, scale=1.0):
    if isinstance(bias, float) and bias != 0.0:
        p.cf(bias)
    if isinstance(scale, float) and scale != 1.0:
        p.cf(scale)
    p.op("scalar", lambda e, o=out, i=in_, f=func, b=bias, s=scale:
         e.activation(out=o, in_=i, func=f, bias=b, scale=s))


def vop(p, name, *args):
    p.op("vector", lambda e, n=name, a=args: getattr(e, n)(*a))


def tt(p, out, a, b, opname):
    p.op("vector", lambda e, o=out, x=a, y=b, op=getattr(ALU, opname):
         e.tensor_tensor(out=o, in0=x, in1=y, op=op))


def ts(p, out, a, s1, opname, s2=None, op2name="bypass"):
    p.op("vector", lambda e, o=out, x=a, s=s1, t=s2, op=getattr(ALU, opname),
         op2=getattr(ALU, op2name):
         e.tensor_scalar(out=o, in0=x, scalar1=s, scalar2=t, op0=op, op1=op2))


def red(p, out, in_, opname, absval=False):
    p.op("vector", lambda e, o=out, i=in_, op=getattr(ALU, opname), av=absval:
         e.tensor_reduce(out=o, in_=i, axis=AX.X, op=op,
                         apply_absolute_value=av))


def mm(p, out, lhsT, rhs, start, stop):
    p.op("tensor", lambda e, o=out, l=lhsT, r=rhs, st=start, sp=stop:
         e.matmul(out=o, lhsT=l, rhs=r, start=st, stop=sp,
                  skip_group_check=True))


def trans(p, out_psum, in_sbuf, eye_sbuf):
    p.op("tensor", lambda e, o=out_psum, i=in_sbuf, y=eye_sbuf:
         e.transpose(out=o, in_=i, identity=y))


def copy(p, out, in_):
    p.op("vector", lambda e, o=out, i=in_: e.tensor_copy(out=o, in_=i))


def scopy(p, out, in_, scale=1.0):
    act(p, out, in_, AF.Identity, bias=0.0, scale=scale)


def memset(p, ap_, val=0.0):
    p.op("vector", lambda e, a=ap_, v=val: e.memset(a, v))


def build_norm(kind, Dd=DM, eps=1e-5):
    def b():
        p = Prog(f"{kind}{Dd}")
        x = p.din("x", [TPC, Dd])
        w = p.din("w", [128, Dd]) if kind == "rms" else None
        y = p.dout("y", [TPC, Dd])
        xt = p.sbuf("xt", [128, Dd])
        wt = p.sbuf("wt", [128, Dd]) if kind == "rms" else None
        m = p.sbuf("m", [128, 1])
        v = p.sbuf("v", [128, 1])
        r = p.sbuf("r", [128, 1])
        sq = p.sbuf("sq", [128, Dd])
        if kind == "rms":
            p.dma("sync", wt, w)
        for t in range(TPC // 128):
            p.dma("sync", xt, x[t * 128:(t + 1) * 128, :])
            if kind == "ln":
                red(p, m, xt, "add")
                ts(p, m, m, 1.0 / Dd, "mult")
                ts(p, xt, xt, m, "subtract")
            act(p, sq, xt, AF.Square)
            red(p, v, sq, "add")
            ts(p, v, v, 1.0 / Dd, "mult")
            act(p, r, v, AF.Sqrt, bias=eps)
            vop(p, "reciprocal", r, r)
            ts(p, xt, xt, r, "mult")
            if kind == "rms":
                tt(p, xt, xt, wt, "mult")
            p.dma("gpsimd", y[t * 128:(t + 1) * 128, :], xt)
        return p
    return cached(("norm", kind, Dd, eps), b)


def launch_norm(kind, x_nat, w=None, eps=1e-5):
    Dd = x_nat.shape[1]
    p = build_norm(kind, Dd, eps)
    ims = []
    for c in range(NCORE):
        im = {"x": np.ascontiguousarray(x_nat[c * TPC:(c + 1) * TPC])}
        if kind == "rms":
            im["w"] = np.ascontiguousarray(
                np.broadcast_to(np.asarray(w)[None, :], (128, Dd)))
        ims.append(im)
    rs = run(p, ims)
    return np.concatenate([r["y"] for r in rs], 0)


def build_gemm(K, M, actname, has_bias, bin_, elu1_blocks=0):
    nkc = (K + 127) // 128
    nmb = (M + 127) // 128
    TSEG = 512
    ntseg = TPC // TSEG
    resident = (K * M * 4) <= 12 * 2 ** 20

    def b():
        p = Prog(f"gemm{K}x{M}{actname}")
        xT = p.din("xT", [K, TPC])
        wT = p.din("wT", [K, M])
        bias = p.din("bias", [128, nmb]) if has_bias else None
        wnat = p.din("wnat", [M, K]) if bin_ else None
        out = p.dout("out", [M, TPC])

        xbuf = p.sbuf("xbuf", [128, nkc * TSEG])
        obuf = p.sbuf("obuf", [128, TSEG])
        t1 = p.sbuf("t1", [128, TSEG])
        t2 = p.sbuf("t2", [128, TSEG])
        ps = p.psum("ps", [128, TSEG])
        bt = None
        if has_bias:
            bt = p.sbuf("bt", [128, nmb])
            p.dma("sync", bt, bias)
        alph = None
        if bin_:
            alph = p.sbuf("alph", [128, nmb])
            wrow = p.sbuf("wrow", [128, K])
            for mb in range(nmb):
                mrows = min(128, M - mb * 128)
                p.dma("sync", wrow[:mrows, :], wnat[mb * 128:mb * 128 + mrows, :])
                red(p, alph[:mrows, mb:mb + 1], wrow[:mrows, :], "add", absval=True)
                ts(p, alph[:mrows, mb:mb + 1], alph[:mrows, mb:mb + 1],
                   1.0 / K, "mult")
        if resident:
            wsb = p.sbuf("wsb", [128, nkc * M])
            dq = ["sync", "gpsimd", "scalar"]
            wv = p.mark()
            for kc in range(nkc):
                krows = min(128, K - kc * 128)
                p.dma(dq[kc % 3], wsb[:krows, kc * M:kc * M + M],
                      wT[kc * 128:kc * 128 + krows, :], wait=wv)
            if bin_:
                for kc in range(nkc):
                    krows = min(128, K - kc * 128)
                    act(p, wsb[:krows, kc * M:(kc + 1) * M],
                        wsb[:krows, kc * M:(kc + 1) * M], AF.Sign)
        else:
            wcol = p.sbuf("wcol", [128, nkc * 128])

        for tg in range(ntseg):
            wv = p.mark()
            for kc in range(nkc):
                krows = min(128, K - kc * 128)
                p.dma("sync" if kc % 2 == 0 else "gpsimd",
                      xbuf[:krows, kc * TSEG:(kc + 1) * TSEG],
                      xT[kc * 128:kc * 128 + krows, tg * TSEG:(tg + 1) * TSEG],
                      wait=wv)
            for mb in range(nmb):
                mrows = min(128, M - mb * 128)
                if not resident:
                    wv = p.mark()
                    dq = ["sync", "gpsimd", "scalar"]
                    for kc in range(nkc):
                        krows = min(128, K - kc * 128)
                        p.dma(dq[kc % 3], wcol[:krows, kc * 128:kc * 128 + mrows],
                              wT[kc * 128:kc * 128 + krows,
                                 mb * 128:mb * 128 + mrows], wait=wv)
                    if bin_:
                        act(p, wcol[:, :], wcol[:, :], AF.Sign)
                for kc in range(nkc):
                    krows = min(128, K - kc * 128)
                    if resident:
                        lhs = wsb[:krows, kc * M + mb * 128:kc * M + mb * 128 + mrows]
                    else:
                        lhs = wcol[:krows, kc * 128:kc * 128 + mrows]
                    mm(p, ps[:mrows, :], lhs, xbuf[:krows, kc * TSEG:(kc + 1) * TSEG],
                       kc == 0, kc == nkc - 1)
                src = ps
                if bin_:
                    ts(p, obuf[:mrows, :], ps[:mrows, :], alph[:mrows, mb:mb + 1],
                       "mult", bt[:mrows, mb:mb + 1] if has_bias else 0.0,
                       "add" if has_bias else "bypass")
                    src = obuf
                elif has_bias:
                    ts(p, obuf[:mrows, :], ps[:mrows, :], bt[:mrows, mb:mb + 1], "add")
                    src = obuf
                if actname == "none":
                    if src is ps:
                        copy(p, obuf[:mrows, :], ps[:mrows, :])
                elif actname == "relu":
                    act(p, obuf[:mrows, :], src[:mrows, :], AF.Relu)
                elif actname == "silu":
                    act(p, obuf[:mrows, :], src[:mrows, :], AF.Silu)
                elif actname == "gelu":
                    act(p, obuf[:mrows, :], src[:mrows, :], AF.Gelu)
                elif actname == "elu1":
                    if mb < elu1_blocks:
                        ts(p, t1[:mrows, :], src[:mrows, :], 0.0, "max")
                        ts(p, t2[:mrows, :], src[:mrows, :], 0.0, "min")
                        act(p, t2[:mrows, :], t2[:mrows, :], AF.Exp)
                        tt(p, obuf[:mrows, :], t1[:mrows, :], t2[:mrows, :], "add")
                    elif src is ps:
                        copy(p, obuf[:mrows, :], ps[:mrows, :])
                else:
                    raise ValueError(actname)
                p.dma("gpsimd", out[mb * 128:mb * 128 + mrows,
                                    tg * TSEG:(tg + 1) * TSEG], obuf[:mrows, :])
        return p
    return cached(("gemm", K, M, actname, has_bias, bin_, elu1_blocks), b)


def launch_gemm(xT_full, wT, actname="none", bias=None, wnat=None, elu1_blocks=0):
    K, M = wT.shape
    nmb = (M + 127) // 128
    p = build_gemm(K, M, actname, bias is not None, wnat is not None, elu1_blocks)
    wTc = np.ascontiguousarray(wT)
    bb = None
    if bias is not None:
        bb = np.zeros((128, nmb), np.float32)
        bv = np.asarray(bias)
        for mb in range(nmb):
            seg = bv[mb * 128:(mb + 1) * 128]
            bb[:len(seg), mb] = seg
    wn = np.ascontiguousarray(wnat) if wnat is not None else None
    ims = []
    for c in range(NCORE):
        im = {"xT": np.ascontiguousarray(xT_full[:, c * TPC:(c + 1) * TPC]),
              "wT": wTc}
        if bb is not None:
            im["bias"] = bb
        if wn is not None:
            im["wnat"] = wn
        ims.append(im)
    rs = run(p, ims)
    return np.concatenate([r["out"] for r in rs], 1)


def build_swa():
    S = 4096
    def b():
        p = Prog("swa")
        qT = p.din("qT", [8, 64, S])
        kT = p.din("kT", [8, 64, S + 256])
        vp = p.din("vp", [8, S + 256, 64])
        msk = p.din("msk", [3, 128, 384])
        eye = p.din("eye", [128, 128])
        o = p.dout("o", [8, S, 64])

        qs = p.sbuf("qs", [64, S])
        ks = p.sbuf("ks", [64, S + 256])
        vs = p.sbuf("vs", [128, 34 * 64])
        ms = p.sbuf("ms", [128, 3 * 384])
        eyes = p.sbuf("eyes", [128, 128])
        sc = p.sbuf("sc", [128, 384])
        mx = p.sbuf("mx", [128, 1])
        sm = p.sbuf("sm", [128, 1])
        pt = p.sbuf("pt", [128, 384])
        ob = p.sbuf("ob", [128, 64])
        ps = p.psum("ps", [128, 384])
        pt_ps = p.psum("pt_ps", [128, 128])
        ops_ = p.psum("ops", [128, 64])

        p.dma("sync", eyes, eye)
        for j in range(3):
            p.dma("sync", ms[:, j * 384:(j + 1) * 384], msk[j])
        for u in range(8):
            wv = p.mark()
            p.dma("sync", qs, qT[u], wait=wv)
            p.dma("gpsimd", ks, kT[u], wait=wv)
            for j in range(34):
                p.dma("scalar", vs[:, j * 64:(j + 1) * 64],
                      vp[u, j * 128:(j + 1) * 128, :], wait=wv)
            for t in range(32):
                mi = 0 if t == 0 else (2 if t == 31 else 1)
                mm(p, ps, qs[:, t * 128:(t + 1) * 128],
                   ks[:, t * 128:t * 128 + 384], True, True)
                scopy(p, sc, ps, scale=0.125)
                tt(p, sc, sc, ms[:, mi * 384:(mi + 1) * 384], "add")
                red(p, mx, sc, "max")
                ts(p, mx, mx, -1.0, "mult")
                act(p, sc, sc, AF.Exp, bias=mx)
                red(p, sm, sc, "add")
                vop(p, "reciprocal", sm, sm)
                ts(p, sc, sc, sm, "mult")
                for j in range(3):
                    trans(p, pt_ps, sc[:, j * 128:(j + 1) * 128], eyes)
                    copy(p, pt[:, j * 128:(j + 1) * 128], pt_ps)
                for j in range(3):
                    mm(p, ops_, pt[:, j * 128:(j + 1) * 128],
                       vs[:, (t + j) * 64:(t + j + 1) * 64], j == 0, j == 2)
                copy(p, ob, ops_)
                p.dma("gpsimd", o[u, t * 128:(t + 1) * 128, :], ob)
        return p
    return cached(("swa",), b)


def launch_swa(qT, kT, v):
    S = 4096
    W = 128
    masks = np.zeros((3, 128, 384), np.float32)
    qi = np.arange(128)[:, None]
    kj = np.arange(384)[None, :] - 128
    band = np.abs(qi - kj) <= W
    for mi, (lo_ok, hi_ok) in enumerate([(False, True), (True, True), (True, False)]):
        valid = band.copy()
        if not lo_ok:
            valid &= (kj >= 0)
        if not hi_ok:
            valid &= (kj < 128)
        masks[mi] = np.where(valid, 0.0, -1e30)
    ims = []
    for c in range(NCORE):
        qTu = np.zeros((8, 64, S), np.float32)
        kTu = np.zeros((8, 64, S + 256), np.float32)
        vu = np.zeros((8, S + 256, 64), np.float32)
        for u in range(8):
            g = c * 8 + u
            bb, hh = g // 16, g % 16
            qTu[u] = qT[hh * 64:(hh + 1) * 64, bb * S:(bb + 1) * S]
            kTu[u, :, 128:128 + S] = kT[hh * 64:(hh + 1) * 64, bb * S:(bb + 1) * S]
            vu[u, 128:128 + S] = v[bb * S:(bb + 1) * S, hh * 64:(hh + 1) * 64]
        ims.append({"qT": qTu, "kT": kTu, "vp": vu, "msk": masks, "eye": _EYE})
    rs = run(build_swa(), ims)
    o = np.zeros((NTOK, DM), np.float32)
    for c in range(NCORE):
        for u in range(8):
            g = c * 8 + u
            bb, hh = g // 16, g % 16
            o[bb * S:(bb + 1) * S, hh * 64:(hh + 1) * 64] = rs[c]["o"][u]
    return o


def build_combine():
    def b():
        p = Prog("combine")
        x = p.din("x", [TPC, DM])
        s = p.din("s", [TPC, DM])
        f = p.din("f", [TPC, DM])
        y = p.dout("y", [TPC, DM])
        xt = p.sbuf("xt", [128, DM])
        st = p.sbuf("st", [128, DM])
        ft = p.sbuf("ft", [128, DM])
        for t in range(TPC // 128):
            sl = slice(t * 128, (t + 1) * 128)
            wv = p.mark()
            p.dma("sync", xt, x[sl, :], wait=wv)
            p.dma("gpsimd", st, s[sl, :], wait=wv)
            p.dma("scalar", ft, f[sl, :], wait=wv)
            act(p, ft, ft, AF.Silu)
            tt(p, ft, ft, st, "mult")
            tt(p, ft, ft, xt, "add")
            p.dma("gpsimd", y[sl, :], ft)
        return p
    return cached(("combine",), b)


def launch_combine(x, s, f):
    p = build_combine()
    ims = [{"x": np.ascontiguousarray(x[c * TPC:(c + 1) * TPC]),
            "s": np.ascontiguousarray(s[c * TPC:(c + 1) * TPC]),
            "f": np.ascontiguousarray(f[c * TPC:(c + 1) * TPC])}
           for c in range(NCORE)]
    rs = run(p, ims)
    return np.concatenate([r["y"] for r in rs], 0)


def build_causal():
    S = 4096
    def b():
        p = Prog("causal")
        qT = p.din("qT", [8, 64, S])
        kT = p.din("kT", [8, 64, S])
        vv = p.din("v", [8, S, 64])
        cm = p.din("cmask", [128, 128])
        eye = p.din("eye", [128, 128])
        o = p.dout("o", [8, S, 64])
        qs = p.sbuf("qs", [64, S])
        ks = p.sbuf("ks", [64, S])
        vs = p.sbuf("vs", [128, 32 * 64])
        cms = p.sbuf("cms", [128, 128])
        eyes = p.sbuf("eyes", [128, 128])
        sc = p.sbuf("sc", [128, S])
        mx = p.sbuf("mx", [128, 1])
        sm = p.sbuf("sm", [128, 1])
        pt = p.sbuf("pt", [128, 128])
        ob = p.sbuf("ob", [128, 64])
        ps = p.psum("ps", [128, 512])
        ptp = p.psum("ptp", [128, 128])
        ops_ = p.psum("ops", [128, 64])
        p.dma("sync", cms, cm)
        p.dma("sync", eyes, eye)
        for u in range(8):
            wv = p.mark()
            p.dma("sync", qs, qT[u], wait=wv)
            p.dma("gpsimd", ks, kT[u], wait=wv)
            for j in range(32):
                p.dma("scalar", vs[:, j * 64:(j + 1) * 64],
                      vv[u, j * 128:(j + 1) * 128, :], wait=wv)
            for t in range(32):
                W = (t + 1) * 128
                for c in range((W + 511) // 512):
                    w = min(512, W - c * 512)
                    mm(p, ps[:, :w], qs[:, t * 128:(t + 1) * 128],
                       ks[:, c * 512:c * 512 + w], True, True)
                    scopy(p, sc[:, c * 512:c * 512 + w], ps[:, :w], scale=0.125)
                tt(p, sc[:, t * 128:W], sc[:, t * 128:W], cms, "add")
                red(p, mx, sc[:, :W], "max")
                ts(p, mx, mx, -1.0, "mult")
                act(p, sc[:, :W], sc[:, :W], AF.Exp, bias=mx)
                red(p, sm, sc[:, :W], "add")
                vop(p, "reciprocal", sm, sm)
                ts(p, sc[:, :W], sc[:, :W], sm, "mult")
                for j in range(t + 1):
                    trans(p, ptp, sc[:, j * 128:(j + 1) * 128], eyes)
                    copy(p, pt, ptp)
                    mm(p, ops_, pt, vs[:, j * 64:(j + 1) * 64], j == 0, j == t)
                copy(p, ob, ops_)
                p.dma("gpsimd", o[u, t * 128:(t + 1) * 128, :], ob)
        return p
    return cached(("causal",), b)


def launch_causal(qT, kT, v, kv_rep=1):
    S = 4096
    cmask = np.where(np.tril(np.ones((128, 128), bool)), 0.0, -1e30
                     ).astype(np.float32)
    ims = []
    for c in range(NCORE):
        qTu = np.zeros((8, 64, S), np.float32)
        kTu = np.zeros((8, 64, S), np.float32)
        vu = np.zeros((8, S, 64), np.float32)
        for u in range(8):
            g = c * 8 + u
            bb, hh = g // 16, g % 16
            kh = hh // kv_rep
            qTu[u] = qT[hh * 64:(hh + 1) * 64, bb * S:(bb + 1) * S]
            kTu[u] = kT[kh * 64:(kh + 1) * 64, bb * S:(bb + 1) * S]
            vu[u] = v[bb * S:(bb + 1) * S, kh * 64:(kh + 1) * 64]
        ims.append({"qT": qTu, "kT": kTu, "v": vu, "cmask": cmask, "eye": _EYE})
    rs = run(build_causal(), ims)
    o = np.zeros((NTOK, DM), np.float32)
    for c in range(NCORE):
        for u in range(8):
            g = c * 8 + u
            bb, hh = g // 16, g % 16
            o[bb * S:(bb + 1) * S, hh * 64:(hh + 1) * 64] = rs[c]["o"][u]
    return o


def build_linattn():
    S = 4096
    def b():
        p = Prog("linattn")
        qT = p.din("qT", [8, 64, S])
        kn = p.din("kn", [8, S, 64])
        vn = p.din("vn", [8, S, 65])
        o = p.dout("o", [8, S, 64])
        qs = p.sbuf("qs", [64, S])
        ks = p.sbuf("ks", [128, 32 * 64])
        vs = p.sbuf("vs", [128, 32 * 65])
        kvs = p.sbuf("kvs", [64, 65])
        zt = p.sbuf("zt", [128, 1])
        ob = p.sbuf("ob", [128, 64])
        pkv = p.psum("pkv", [64, 65])
        po = p.psum("po", [128, 65])
        for u in range(8):
            wv = p.mark()
            p.dma("sync", qs, qT[u], wait=wv)
            for j in range(32):
                p.dma("gpsimd", ks[:, j * 64:(j + 1) * 64],
                      kn[u, j * 128:(j + 1) * 128, :], wait=wv)
                p.dma("scalar", vs[:, j * 65:(j + 1) * 65],
                      vn[u, j * 128:(j + 1) * 128, :], wait=wv)
            for j in range(32):
                mm(p, pkv, ks[:, j * 64:(j + 1) * 64], vs[:, j * 65:(j + 1) * 65],
                   j == 0, j == 31)
            copy(p, kvs, pkv)
            for t in range(32):
                mm(p, po, qs[:, t * 128:(t + 1) * 128], kvs, True, True)
                ts(p, zt, po[:, 64:65], 1e-6, "add")
                vop(p, "reciprocal", zt, zt)
                ts(p, ob, po[:, 0:64], zt, "mult")
                p.dma("gpsimd", o[u, t * 128:(t + 1) * 128, :], ob)
        return p
    return cached(("linattn",), b)


def launch_linattn(qT, k, v):
    S = 4096
    ims = []
    for c in range(NCORE):
        qTu = np.zeros((8, 64, S), np.float32)
        knu = np.zeros((8, S, 64), np.float32)
        vnu = np.zeros((8, S, 65), np.float32)
        vnu[:, :, 64] = 1.0
        for u in range(8):
            g = c * 8 + u
            bb, hh = g // 16, g % 16
            qTu[u] = qT[hh * 64:(hh + 1) * 64, bb * S:(bb + 1) * S]
            knu[u] = k[bb * S:(bb + 1) * S, hh * 64:(hh + 1) * 64]
            vnu[u, :, :64] = v[bb * S:(bb + 1) * S, hh * 64:(hh + 1) * 64]
        ims.append({"qT": qTu, "kn": knu, "vn": vnu})
    rs = run(build_linattn(), ims)
    o = np.zeros((NTOK, DM), np.float32)
    for c in range(NCORE):
        for u in range(8):
            g = c * 8 + u
            bb, hh = g // 16, g % 16
            o[bb * S:(bb + 1) * S, hh * 64:(hh + 1) * 64] = rs[c]["o"][u]
    return o


def build_router(mode, E=8):
    def b():
        p = Prog(f"router{mode}")
        xT = p.din("xT", [DM, TPC])
        rw = p.din("rw", [DM, E])
        br = p.din("br", [128, E]) if mode == "softmax" else None
        c = p.dout("c", [TPC, E])
        xb = p.sbuf("xb", [128, 128])
        rws = p.sbuf("rws", [128, 8 * E])
        brs = p.sbuf("brs", [128, E]) if mode == "softmax" else None
        lg = p.sbuf("lg", [128, E])
        m1 = p.sbuf("m1", [128, 1])
        m2 = p.sbuf("m2", [128, 1])
        i1 = p.sbuf("i1", [128, E])
        i2 = p.sbuf("i2", [128, E])
        w1 = p.sbuf("w1", [128, 1])
        w2 = p.sbuf("w2", [128, 1])
        pm = p.sbuf("pm", [128, E])
        ps = p.psum("ps", [128, E])
        wv = p.mark()
        for kc in range(8):
            p.dma("sync", rws[:, kc * E:(kc + 1) * E],
                  rw[kc * 128:(kc + 1) * 128, :], wait=wv)
        if mode == "softmax":
            p.dma("gpsimd", brs, br, wait=wv)
        for t in range(TPC // 128):
            wv = p.mark()
            for kc in range(8):
                p.dma("sync", xb, xT[kc * 128:(kc + 1) * 128,
                                     t * 128:(t + 1) * 128], wait=None)
                mm(p, ps, xb, rws[:, kc * E:(kc + 1) * E], kc == 0, kc == 7)
            if mode == "softmax":
                tt(p, lg, ps, brs, "add")
                red(p, m1, lg, "max")
                ts(p, m1, m1, -1.0, "mult")
                act(p, lg, lg, AF.Exp, bias=m1)
                red(p, m2, lg, "add")
                vop(p, "reciprocal", m2, m2)
                ts(p, lg, lg, m2, "mult")
            elif mode == "top2prob":
                red(p, m1, ps, "max")
                ts(p, m1, m1, -1.0, "mult")
                act(p, lg, ps, AF.Exp, bias=m1)
                red(p, m2, lg, "add")
                vop(p, "reciprocal", m2, m2)
                ts(p, lg, lg, m2, "mult")
                red(p, m1, lg, "max")
                ts(p, i1, lg, m1, "is_ge")
                p.op("vector", lambda e, o=pm, a=i1, s=-1e30, bb_=lg:
                     e.scalar_tensor_tensor(out=o, in0=a, scalar=s, in1=bb_,
                                            op0=ALU.mult, op1=ALU.add))
                red(p, m2, pm, "max")
                ts(p, i2, lg, m2, "is_ge")
                tt(p, lg, lg, i2, "mult")
            else:
                copy(p, lg, ps)
                red(p, m1, lg, "max")
                ts(p, i1, lg, m1, "is_ge")
                p.op("vector", lambda e, o=pm, a=i1, s=-1e30, bb_=lg:
                     e.scalar_tensor_tensor(out=o, in0=a, scalar=s, in1=bb_,
                                            op0=ALU.mult, op1=ALU.add))
                red(p, m2, pm, "max")
                ts(p, i2, lg, m2, "is_ge")
                tt(p, i2, i2, i1, "subtract")
                tt(p, w1, m1, m2, "subtract")
                act(p, w1, w1, AF.Sigmoid)
                ts(p, w2, w1, -1.0, "mult", 1.0, "add")
                ts(p, i1, i1, w1, "mult")
                ts(p, i2, i2, w2, "mult")
                tt(p, lg, i1, i2, "add")
            p.dma("gpsimd", c[t * 128:(t + 1) * 128, :], lg)
        return p
    return cached(("router", mode, E), b)


def launch_router(mode, xT, rw, br=None):
    E = rw.shape[1]
    p = build_router(mode, E)
    ims = []
    for c in range(NCORE):
        im = {"xT": np.ascontiguousarray(xT[:, c * TPC:(c + 1) * TPC]),
              "rw": np.ascontiguousarray(rw)}
        if mode == "softmax":
            im["br"] = np.ascontiguousarray(
                np.broadcast_to(np.asarray(br)[None, :], (128, E)))
        ims.append(im)
    rs = run(p, ims)
    return np.concatenate([r["c"] for r in rs], 0)


def build_bisect(kk=2048, iters=45):
    def b():
        p = Prog("bisect")
        sm = p.din("sm", [128, 128])
        ones = p.din("ones", [128, 128])
        c = p.dout("c", [128, 128])
        sms = p.sbuf("sms", [128, 128])
        ons = p.sbuf("ons", [128, 128])
        lo = p.sbuf("lo", [1, 1])
        hi = p.sbuf("hi", [1, 1])
        mid = p.sbuf("mid", [1, 1])
        tq = p.sbuf("tq", [1, 1])
        cond = p.sbuf("cond", [1, 1])
        cond2 = p.sbuf("cond2", [1, 1])
        cnt = p.sbuf("cnt", [1, 1])
        mrep = p.sbuf("mrep", [128, 1])
        ind = p.sbuf("ind", [128, 128])
        cs = p.sbuf("cs", [128, 1])
        pr = p.psum("pr", [128, 1])
        pc = p.psum("pc", [1, 1])
        wv = p.mark()
        p.dma("sync", sms, sm, wait=wv)
        p.dma("gpsimd", ons, ones, wait=wv)
        memset(p, lo, 0.0)
        memset(p, hi, 1.0)
        for it in range(iters):
            tt(p, mid, lo, hi, "add")
            ts(p, mid, mid, 0.5, "mult")
            mm(p, pr, ons[0:1, :], mid, True, True)
            copy(p, mrep, pr)
            ts(p, ind, sms, mrep, "is_ge")
            red(p, cs, ind, "add")
            mm(p, pc, cs, ons[:, 0:1], True, True)
            copy(p, cnt, pc)
            ts(p, cond, cnt, float(kk), "is_ge")
            ts(p, cond2, cond, -1.0, "mult", 1.0, "add")
            tt(p, tq, mid, lo, "subtract")
            tt(p, tq, tq, cond, "mult")
            tt(p, lo, lo, tq, "add")
            tt(p, tq, mid, hi, "subtract")
            tt(p, tq, tq, cond2, "mult")
            tt(p, hi, hi, tq, "add")
        mm(p, pr, ons[0:1, :], lo, True, True)
        copy(p, mrep, pr)
        ts(p, ind, sms, mrep, "is_ge")
        tt(p, ind, ind, sms, "mult")
        p.dma("gpsimd", c, ind)
        return p
    return cached(("bisect", kk, iters), b)


def launch_bisect(Sm):
    ones = np.ones((128, 128), np.float32)
    ims = [{"sm": np.ascontiguousarray(Sm[:, e].reshape(128, 128)),
            "ones": ones} for e in range(8)]
    rs = run(build_bisect(), ims)
    c = np.zeros((NTOK, 8), np.float32)
    for e in range(8):
        c[:, e] = rs[e]["c"].reshape(-1)
    return c


def build_conv(C, l2heads):
    def b():
        p = Prog(f"conv{C}")
        xp = p.din("xp", [TPC + 3, C])
        wr = p.din("wr", [128, 4 * C])
        y = p.dout("y", [TPC, C])
        xb = [p.sbuf(f"xb{j}", [128, C]) for j in range(4)]
        wrs = p.sbuf("wrs", [128, 4 * C])
        acc = p.sbuf("acc", [128, C])
        tmp = p.sbuf("tmp", [128, C])
        sq = p.sbuf("sq", [128, C])
        nh = C // 64
        ss = p.sbuf("ss", [128, max(nh, 1)])
        p.dma("sync", wrs, wr)
        for t in range(TPC // 128):
            wv = p.mark()
            for j in range(4):
                p.dma(["sync", "gpsimd", "scalar"][j % 3], xb[j],
                      xp[t * 128 + j:t * 128 + j + 128, :], wait=wv)
            tt(p, acc, xb[0], wrs[:, 0:C], "mult")
            for j in range(1, 4):
                tt(p, tmp, xb[j], wrs[:, j * C:(j + 1) * C], "mult")
                tt(p, acc, acc, tmp, "add")
            act(p, acc, acc, AF.Silu)
            if l2heads:
                act(p, sq, acc, AF.Square)
                for h in range(nh):
                    red(p, ss[:, h:h + 1], sq[:, h * 64:(h + 1) * 64], "add")
                act(p, ss, ss, AF.Sqrt)
                ts(p, ss, ss, 1e-12, "max")
                vop(p, "reciprocal", ss, ss)
                for h in range(nh):
                    ts(p, acc[:, h * 64:(h + 1) * 64],
                       acc[:, h * 64:(h + 1) * 64], ss[:, h:h + 1], "mult")
            p.dma("gpsimd", y[t * 128:(t + 1) * 128, :], acc)
        return p
    return cached(("conv", C, l2heads), b)


def launch_conv(x_nat, w, l2heads):
    C = x_nat.shape[1]
    p = build_conv(C, l2heads)
    wrep = np.zeros((128, 4 * C), np.float32)
    for j in range(4):
        wrep[:, j * C:(j + 1) * C] = np.asarray(w)[:, j][None, :]
    ims = []
    for c in range(NCORE):
        start = c * TPC
        xp = np.zeros((TPC + 3, C), np.float32)
        if start % 4096 == 0:
            xp[3:] = x_nat[start:start + TPC]
        else:
            xp[:] = x_nat[start - 3:start + TPC]
        ims.append({"xp": xp, "wr": wrep})
    rs = run(p, ims)
    return np.concatenate([r["y"] for r in rs], 0)


def build_scanprep():
    NH = 12
    def b():
        p = Prog("scanprep")
        gr = p.din("gr", [TPC, NH])
        br = p.din("br", [TPC, NH])
        al = p.din("al", [1, NH])
        ones = p.din("ones", [1, 128])
        gk = p.dout("gk", [TPC, NH])
        be = p.dout("be", [TPC, NH])
        als = p.sbuf("als", [1, NH])
        ons = p.sbuf("ons", [1, 128])
        narep = p.sbuf("narep", [128, NH])
        gb = p.sbuf("gb", [128, NH])
        bb_ = p.sbuf("bb", [128, NH])
        r0 = p.sbuf("r0", [128, NH])
        ab = p.sbuf("ab", [128, NH])
        pn = p.psum("pn", [128, NH])
        wv = p.mark()
        p.dma("sync", als, al, wait=wv)
        p.dma("gpsimd", ons, ones, wait=wv)
        act(p, als, als, AF.Exp)
        ts(p, als, als, -1.0, "mult")
        mm(p, pn, ons, als, True, True)
        copy(p, narep, pn)
        for t in range(TPC // 128):
            sl = slice(t * 128, (t + 1) * 128)
            wv = p.mark()
            p.dma("sync", gb, gr[sl, :], wait=wv)
            p.dma("gpsimd", bb_, br[sl, :], wait=wv)
            ts(p, r0, gb, 0.0, "max")
            act(p, ab, gb, AF.Abs)
            act(p, ab, ab, AF.Exp, scale=-1.0)
            act(p, ab, ab, AF.Ln, bias=1.0)
            tt(p, r0, r0, ab, "add")
            tt(p, r0, r0, narep, "mult")
            p.dma("gpsimd", gk[sl, :], r0)
            act(p, bb_, bb_, AF.Sigmoid)
            p.dma("scalar", be[sl, :], bb_)
        return p
    return cached(("scanprep",), b)


def launch_scanprep(graw, braw, A_log):
    p = build_scanprep()
    ones = np.ones((1, 128), np.float32)
    al = np.asarray(A_log, np.float32).reshape(1, 12)
    ims = [{"gr": np.ascontiguousarray(graw[c * TPC:(c + 1) * TPC]),
            "br": np.ascontiguousarray(braw[c * TPC:(c + 1) * TPC]),
            "al": al, "ones": ones} for c in range(NCORE)]
    rs = run(p, ims)
    gk = np.concatenate([r["gk"] for r in rs], 0)
    be = np.concatenate([r["be"] for r in rs], 0)
    return gk, be


def build_scan():
    S, CH, DK, DV = 4096, 128, 64, 128
    NCHUNK = S // CH
    def b():
        p = Prog("scan")
        qT = p.din("qT", [6, DK, S])
        kT = p.din("kT", [6, DK, S])
        kn = p.din("kn", [6, S, DK])
        vn = p.din("vn", [6, S, DV])
        g6 = p.din("g6", [6, S])
        b6 = p.din("b6", [6, S])
        eye = p.din("eye", [128, 128])
        ltt = p.din("ltt", [128, 128])
        sln = p.din("sln", [128, 128])
        il = p.din("il", [128, 128])
        ones = p.din("ones", [128, 128])
        o6 = p.dout("o6", [6, S, DV])

        eyes = p.sbuf("eyes", [128, 128])
        ltts = p.sbuf("ltts", [128, 128])
        slns = p.sbuf("slns", [128, 128])
        ils = p.sbuf("ils", [128, 128])
        ons = p.sbuf("ons", [128, 128])
        S_ = p.sbuf("S_", [DK, DV])
        qt = p.sbuf("qt", [DK, CH])
        kt = p.sbuf("kt", [DK, CH])
        knt = p.sbuf("knt", [CH, DK])
        vt = p.sbuf("vt", [CH, DV])
        gcol = p.sbuf("gcol", [CH, 1])
        bcol = p.sbuf("bcol", [CH, 1])
        G = p.sbuf("G", [CH, 1])
        Acol = p.sbuf("Acol", [CH, 1])
        GC = p.sbuf("GC", [1, 1])
        GCr = p.sbuf("GCr", [CH, 1])
        wcol = p.sbuf("wcol", [CH, 1])
        Grow = p.sbuf("Grow", [1, CH])
        expo = p.sbuf("expo", [CH, CH])
        Dm = p.sbuf("Dm", [CH, CH])
        Nm = p.sbuf("Nm", [CH, CH])
        Tm = p.sbuf("Tm", [CH, CH])
        Pm = p.sbuf("Pm", [CH, CH])
        PTm = p.sbuf("PTm", [CH, CH])
        RHS = p.sbuf("RHS", [CH, DV])
        Um = p.sbuf("Um", [CH, DV])
        PPm = p.sbuf("PPm", [CH, CH])
        Ob = p.sbuf("Ob", [CH, DV])
        Kd = p.sbuf("Kd", [CH, DK])
        bA = p.sbuf("bA", [CH, 1])
        bV = p.sbuf("bV", [CH, DV])
        GC64 = p.sbuf("GC64", [DK, 1])
        pA = p.psum("pA", [128, 128])
        pB = p.psum("pB", [128, 128])
        pS = p.psum("pS", [DK, DV])
        pV = p.psum("pV", [128, 1])
        pW = p.psum("pW", [DK, 1])

        wv = p.mark()
        p.dma("sync", eyes, eye, wait=wv)
        p.dma("sync", ltts, ltt, wait=wv)
        p.dma("gpsimd", slns, sln, wait=wv)
        p.dma("gpsimd", ils, il, wait=wv)
        p.dma("scalar", ons, ones, wait=wv)
        for u in range(6):
            memset(p, S_, 0.0)
            for c in range(NCHUNK):
                sl = slice(c * CH, (c + 1) * CH)
                wv = p.mark()
                p.dma("sync", qt, qT[u, :, sl], wait=wv)
                p.dma("sync", kt, kT[u, :, sl], wait=wv)
                p.dma("gpsimd", knt, kn[u, sl, :], wait=wv)
                p.dma("gpsimd", vt, vn[u, sl, :], wait=wv)
                p.dma("scalar", gcol, g6[u, sl][:, None], wait=wv)
                p.dma("scalar", bcol, b6[u, sl][:, None], wait=wv)
                mm(p, pV, ltts, gcol, True, True)
                copy(p, G, pV)
                act(p, Acol, G, AF.Exp)
                mm(p, pW[0:1, :], gcol, ons[:, 0:1], True, True)
                copy(p, GC, pW[0:1, :])
                mm(p, pV, ons[0:1, :], GC, True, True)
                copy(p, GCr, pV)
                act(p, wcol, G, AF.Exp, scale=-1.0, bias=GCr)
                mm(p, pA[0:1, :], G, eyes, True, True)
                copy(p, Grow, pA[0:1, :])
                mm(p, pA, ons[0:1, :], Grow, True, True)
                ts(p, expo, pA, G, "subtract", 0.0, "max")
                act(p, Dm, expo, AF.Exp, scale=-1.0)
                mm(p, pA, kt, kt, True, True)
                tt(p, Nm, Dm, pA, "mult")
                ts(p, Nm, Nm, bcol, "mult")
                tt(p, Nm, Nm, slns, "mult")
                tt(p, Tm, Nm, eyes, "add")
                copy(p, Pm, Nm)
                for lev in range(6):
                    trans(p, pB, Pm, eyes)
                    copy(p, PTm, pB)
                    mm(p, pA, PTm, Pm, True, True)
                    copy(p, Pm, pA)
                    trans(p, pB, Tm, eyes)
                    copy(p, PTm, pB)
                    mm(p, pA, PTm, Pm, True, True)
                    tt(p, Tm, Tm, pA, "add")
                mm(p, pA[:, 0:DV], kt, S_, True, True)
                tt(p, bA, bcol, Acol, "mult")
                ts(p, bA, bA, -1.0, "mult")
                ts(p, bV, vt, bcol, "mult")
                p.op("vector", lambda e, o=RHS, i0=pA[:, 0:DV], s=bA, i1=bV:
                     e.scalar_tensor_tensor(out=o, in0=i0, scalar=s, in1=i1,
                                            op0=ALU.mult, op1=ALU.add))
                trans(p, pB, Tm, eyes)
                copy(p, PTm, pB)
                mm(p, pA[:, 0:DV], PTm, RHS, True, True)
                copy(p, Um, pA[:, 0:DV])
                mm(p, pA, qt, kt, True, True)
                tt(p, PPm, Dm, pA, "mult")
                tt(p, PPm, PPm, ils, "mult")
                mm(p, pA[:, 0:DV], qt, S_, True, True)
                ts(p, Ob, pA[:, 0:DV], Acol, "mult")
                trans(p, pB, PPm, eyes)
                copy(p, PTm, pB)
                mm(p, pA[:, 0:DV], PTm, Um, True, True)
                tt(p, Ob, Ob, pA[:, 0:DV], "add")
                p.dma("gpsimd", o6[u, sl, :], Ob)
                ts(p, Kd, knt, wcol, "mult")
                mm(p, pS, Kd, Um, True, True)
                mm(p, pW, ons[0:1, 0:DK], GC, True, True)
                copy(p, GC64, pW)
                act(p, GC64, GC64, AF.Exp)
                ts(p, S_, S_, GC64, "mult")
                tt(p, S_, S_, pS, "add")
        return p
    return cached(("scan",), b)


def launch_scan(qT, kT, k, v, gk, beta):
    S = 4096
    ltt = np.ascontiguousarray(np.tril(np.ones((128, 128), np.float32)).T)
    sln = np.where(np.tril(np.ones((128, 128), bool), -1), -1.0, 0.0
                   ).astype(np.float32)
    il = np.tril(np.ones((128, 128), np.float32))
    ones = np.ones((128, 128), np.float32)
    ims = []
    for c in range(NCORE):
        qTu = np.zeros((6, 64, S), np.float32)
        kTu = np.zeros((6, 64, S), np.float32)
        knu = np.zeros((6, S, 64), np.float32)
        vnu = np.zeros((6, S, 128), np.float32)
        g6 = np.zeros((6, S), np.float32)
        b6 = np.zeros((6, S), np.float32)
        for u in range(6):
            g = c * 6 + u
            bb, hh = g // 12, g % 12
            qTu[u] = qT[hh * 64:(hh + 1) * 64, bb * S:(bb + 1) * S]
            kTu[u] = kT[hh * 64:(hh + 1) * 64, bb * S:(bb + 1) * S]
            knu[u] = k[bb * S:(bb + 1) * S, hh * 64:(hh + 1) * 64]
            vnu[u] = v[bb * S:(bb + 1) * S, hh * 128:(hh + 1) * 128]
            g6[u] = gk[bb * S:(bb + 1) * S, hh]
            b6[u] = beta[bb * S:(bb + 1) * S, hh]
        ims.append({"qT": qTu, "kT": kTu, "kn": knu, "vn": vnu,
                    "g6": g6, "b6": b6, "eye": _EYE, "ltt": ltt,
                    "sln": sln, "il": il, "ones": ones})
    rs = run(build_scan(), ims)
    o = np.zeros((NTOK, 1536), np.float32)
    for c in range(NCORE):
        for u in range(6):
            g = c * 6 + u
            bb, hh = g // 12, g % 12
            o[bb * S:(bb + 1) * S, hh * 128:(hh + 1) * 128] = rs[c]["o6"][u]
    return o


def build_opost():
    C = 1536
    def b():
        p = Prog("opost")
        o = p.din("o", [TPC, C])
        gp = p.din("gp", [TPC, C])
        gw = p.din("gw", [128, C])
        y = p.dout("y", [TPC, C])
        ot = p.sbuf("ot", [128, C])
        gt = p.sbuf("gt", [128, C])
        gws = p.sbuf("gws", [128, C])
        sq = p.sbuf("sq", [128, C])
        ss = p.sbuf("ss", [128, 12])
        p.dma("sync", gws, gw)
        for t in range(TPC // 128):
            sl = slice(t * 128, (t + 1) * 128)
            wv = p.mark()
            p.dma("sync", ot, o[sl, :], wait=wv)
            p.dma("gpsimd", gt, gp[sl, :], wait=wv)
            act(p, sq, ot, AF.Square)
            for h in range(12):
                red(p, ss[:, h:h + 1], sq[:, h * 128:(h + 1) * 128], "add")
            ts(p, ss, ss, 1.0 / 128, "mult")
            act(p, ss, ss, AF.Sqrt, bias=1e-5)
            vop(p, "reciprocal", ss, ss)
            for h in range(12):
                ts(p, ot[:, h * 128:(h + 1) * 128],
                   ot[:, h * 128:(h + 1) * 128], ss[:, h:h + 1], "mult")
            tt(p, ot, ot, gws, "mult")
            act(p, gt, gt, AF.Silu)
            tt(p, ot, ot, gt, "mult")
            p.dma("gpsimd", y[sl, :], ot)
        return p
    return cached(("opost",), b)


def launch_opost(o, gproj, gnorm_w):
    p = build_opost()
    gw = np.zeros((128, 1536), np.float32)
    gw[:] = np.tile(np.asarray(gnorm_w, np.float32), 12)[None, :]
    ims = [{"o": np.ascontiguousarray(o[c * TPC:(c + 1) * TPC]),
            "gp": np.ascontiguousarray(gproj[c * TPC:(c + 1) * TPC]),
            "gw": gw} for c in range(NCORE)]
    rs = run(p, ims)
    return np.concatenate([r["y"] for r in rs], 0)


def build_ew(variant, E=8):
    def b():
        p = Prog(f"ew{variant}")
        y = p.dout("y", [TPC, DM])
        at = p.sbuf("at", [128, DM])
        bt = p.sbuf("bt", [128, DM])
        acc = p.sbuf("acc", [128, DM])
        if variant == "add2":
            a = p.din("a", [TPC, DM])
            bb_ = p.din("b", [TPC, DM])
            for t in range(TPC // 128):
                sl = slice(t * 128, (t + 1) * 128)
                wv = p.mark()
                p.dma("sync", at, a[sl, :], wait=wv)
                p.dma("gpsimd", bt, bb_[sl, :], wait=wv)
                tt(p, acc, at, bt, "add")
                p.dma("gpsimd", y[sl, :], acc)
            return p
        o8 = p.din("o8", [E, TPC, DM])
        cc = p.din("c", [TPC, E])
        ct = p.sbuf("ct", [128, E])
        ot = [p.sbuf(f"ot{e}", [128, DM]) for e in range(E)]
        if variant == "moesilu":
            x = p.din("x", [TPC, DM])
            s = p.din("s", [TPC, DM])
        else:
            x = p.din("base", [TPC, DM])
        for t in range(TPC // 128):
            sl = slice(t * 128, (t + 1) * 128)
            wv = p.mark()
            qs = ["sync", "gpsimd", "scalar"]
            for e in range(E):
                p.dma(qs[e % 3], ot[e], o8[e, sl.start:sl.stop, :], wait=wv)
            p.dma("sync", ct, cc[sl, :], wait=wv)
            p.dma("gpsimd", at, x[sl, :], wait=wv)
            if variant == "moesilu":
                p.dma("scalar", bt, s[sl, :], wait=wv)
            ts(p, acc, ot[0], ct[:, 0:1], "mult")
            for e in range(1, E):
                p.op("vector", lambda ee, o=acc, i0=ot[e], sc=ct[:, e:e + 1],
                     i1=acc: ee.scalar_tensor_tensor(
                         out=o, in0=i0, scalar=sc, in1=i1,
                         op0=ALU.mult, op1=ALU.add))
            if variant == "moesilu":
                act(p, acc, acc, AF.Silu)
                tt(p, acc, acc, bt, "mult")
            tt(p, acc, acc, at, "add")
            p.dma("gpsimd", y[sl, :], acc)
        return p
    return cached(("ew", variant, E), b)


def launch_ew(variant, o8=None, c=None, x=None, s=None, a=None, b=None):
    p = build_ew(variant)
    ims = []
    for cix in range(NCORE):
        sl = slice(cix * TPC, (cix + 1) * TPC)
        if variant == "add2":
            ims.append({"a": np.ascontiguousarray(a[sl]),
                        "b": np.ascontiguousarray(b[sl])})
        else:
            im = {"o8": np.ascontiguousarray(np.stack(
                      [oe[sl] for oe in o8], 0)),
                  "c": np.ascontiguousarray(c[sl])}
            if variant == "moesilu":
                im["x"] = np.ascontiguousarray(x[sl])
                im["s"] = np.ascontiguousarray(s[sl])
            else:
                im["base"] = np.ascontiguousarray(x[sl])
            ims.append(im)
    rs = run(p, ims)
    return np.concatenate([r["y"] for r in rs], 0)


def build_moe_chain(E, two_stage, has_base):
    K = DM
    M1 = DM
    M2 = DM
    TSEG = 512
    ntseg = TPC // TSEG
    def b():
        p = Prog(f"moe{two_stage}{has_base}")
        xT = p.din("xT", [K, TPC])
        w1 = p.din("w1", [E, K, M1])
        w2 = p.din("w2", [E, M1, M2]) if two_stage else None
        cT = p.din("cT", [E, TPC])
        x = p.din("x", [TPC, DM])
        s = p.din("s", [TPC, DM])
        base = p.din("base", [TPC, DM]) if has_base else None
        eye = p.din("eye", [128, 128])
        ones = p.din("ones", [1, 128])
        y = p.dout("y", [TPC, DM])

        xbuf = p.sbuf("xbuf", [128, 8 * TSEG])
        w1s = p.sbuf("w1s", [128, 8 * M1])
        w2s = p.sbuf("w2s", [128, 8 * M2]) if two_stage else None
        hs = p.sbuf("hs", [128, 8 * TSEG]) if two_stage else None
        ce = p.sbuf("ce", [1, TPC])
        crep = p.sbuf("crep", [128, TSEG])
        ob = p.sbuf("ob", [128, TSEG])
        yT_sb = p.sbuf("yT_sb", [128, 8 * TPC])
        xt = p.sbuf("xt", [128, DM])
        st = p.sbuf("st", [128, DM])
        bt = p.sbuf("bt", [128, DM]) if has_base else None
        yn = p.sbuf("yn", [128, DM])
        eyes = p.sbuf("eyes", [128, 128])
        ons = p.sbuf("ons", [1, 128])
        psb = [p.psum(f"psb{i}", [128, 512]) for i in range(8)]
        ptr = psb[0][:, 0:128]
        p.dma("sync", eyes, eye)
        p.dma("sync", ons, ones)
        for e in range(E):
            wv = p.mark()
            for kc in range(8):
                p.dma(["sync", "gpsimd", "scalar"][kc % 3],
                      w1s[:, kc * M1:(kc + 1) * M1], w1[e, kc * 128:(kc + 1) * 128, :],
                      wait=wv)
                if two_stage:
                    p.dma(["gpsimd", "scalar", "sync"][kc % 3],
                          w2s[:, kc * M2:(kc + 1) * M2],
                          w2[e, kc * 128:(kc + 1) * 128, :], wait=wv)
            p.dma("sync", ce, cT[e:e + 1, :], wait=wv)
            for tg in range(ntseg):
                wv = p.mark()
                for kc in range(8):
                    p.dma(["sync", "gpsimd", "scalar"][kc % 3],
                          xbuf[:, kc * TSEG:(kc + 1) * TSEG],
                          xT[kc * 128:(kc + 1) * 128, tg * TSEG:(tg + 1) * TSEG],
                          wait=wv)
                mm(p, psb[0][:, 0:TSEG], ons,
                   ce[0:1, tg * TSEG:(tg + 1) * TSEG], True, True)
                copy(p, crep, psb[0][:, 0:TSEG])
                for kc in range(8):
                    for mb in range(8):
                        mm(p, psb[mb][:, 0:TSEG],
                           w1s[:, kc * M1 + mb * 128:kc * M1 + (mb + 1) * 128],
                           xbuf[:, kc * TSEG:(kc + 1) * TSEG],
                           kc == 0, kc == 7)
                if two_stage:
                    for mb in range(8):
                        act(p, hs[:, mb * TSEG:(mb + 1) * TSEG],
                            psb[mb][:, 0:TSEG], AF.Relu)
                    for kc in range(8):
                        for mb in range(8):
                            mm(p, psb[mb][:, 0:TSEG],
                               w2s[:, kc * M2 + mb * 128:kc * M2 + (mb + 1) * 128],
                               hs[:, kc * TSEG:(kc + 1) * TSEG],
                               kc == 0, kc == 7)
                for mb in range(8):
                    tt(p, ob, psb[mb][:, 0:TSEG], crep, "mult")
                    if e == 0:
                        copy(p, yT_sb[:, mb * TPC + tg * TSEG:
                                      mb * TPC + (tg + 1) * TSEG], ob)
                    else:
                        tt(p, yT_sb[:, mb * TPC + tg * TSEG:
                                    mb * TPC + (tg + 1) * TSEG],
                           yT_sb[:, mb * TPC + tg * TSEG:
                                 mb * TPC + (tg + 1) * TSEG], ob, "add")
        for t in range(TPC // 128):
            wv = p.mark()
            p.dma("sync", xt, x[t * 128:(t + 1) * 128, :], wait=wv)
            p.dma("gpsimd", st, s[t * 128:(t + 1) * 128, :], wait=wv)
            if has_base:
                p.dma("scalar", bt, base[t * 128:(t + 1) * 128, :], wait=wv)
            for mb in range(8):
                trans(p, ptr, yT_sb[:, mb * TPC + t * 128:
                                    mb * TPC + (t + 1) * 128], eyes)
                copy(p, yn[:, mb * 128:(mb + 1) * 128], ptr)
            if has_base:
                tt(p, yn, yn, bt, "add")
            act(p, yn, yn, AF.Silu)
            tt(p, yn, yn, st, "mult")
            tt(p, yn, yn, xt, "add")
            p.dma("gpsimd", y[t * 128:(t + 1) * 128, :], yn)
        return p
    return cached(("moechain", E, two_stage, has_base), b)


def launch_moe_chain(xT, w1, w2, c, x, s, base=None):
    two = w2 is not None
    p = build_moe_chain(8, two, base is not None)
    w1c = np.ascontiguousarray(np.asarray(w1, np.float32))
    w2c = np.ascontiguousarray(np.asarray(w2, np.float32)) if two else None
    cTc = np.ascontiguousarray(np.asarray(c, np.float32).T)
    ims = []
    for ci in range(NCORE):
        sl = slice(ci * TPC, (ci + 1) * TPC)
        im = {"xT": np.ascontiguousarray(xT[:, sl]), "w1": w1c,
              "cT": np.ascontiguousarray(cTc[:, sl]),
              "x": np.ascontiguousarray(x[sl]),
              "s": np.ascontiguousarray(s[sl]), "eye": _EYE,
              "ones": _ONES_ROW}
        if two:
            im["w2"] = w2c
        if base is not None:
            im["base"] = np.ascontiguousarray(base[sl])
        ims.append(im)
    rs = run(p, ims)
    return np.concatenate([r["y"] for r in rs], 0)


def build_moe2():
    K = DM
    F = 2730
    NF = 22
    TSEG = 512
    ntseg = TPC // TSEG
    def b():
        p = Prog("moe2")
        xT = p.din("xT", [K, TPC])
        wu = p.din("wu", [4, K, F])
        bu = p.din("bu", [4, 128, NF])
        wd = p.din("wd", [8, F, DM])
        bd = p.din("bd", [8, 128, 8])
        cT = p.din("cT", [8, TPC])
        x = p.din("x", [TPC, DM])
        s = p.din("s", [TPC, DM])
        eye = p.din("eye", [128, 128])
        ones = p.din("ones", [1, 128])
        y = p.dout("y", [TPC, DM])
        midT = p.nc.dram_tensor("midscr", [4, F, TPC], F32, kind="Internal").ap()

        xbuf = p.sbuf("xbuf", [128, 8 * TSEG])
        wub = p.sbuf("wub", [128, max(8 * F, NF * DM)])
        bus = p.sbuf("bus", [128, NF])
        bds = p.sbuf("bds", [128, 8])
        mbuf = p.sbuf("mbuf", [128, TSEG])
        ce = p.sbuf("ce", [1, TPC])
        crep = p.sbuf("crep", [128, TSEG])
        ob = p.sbuf("ob", [128, TSEG])
        yT_sb = p.sbuf("yT_sb", [128, 8 * TPC])
        xt = p.sbuf("xt", [128, DM])
        st = p.sbuf("st", [128, DM])
        yn = p.sbuf("yn", [128, DM])
        eyes = p.sbuf("eyes", [128, 128])
        ons = p.sbuf("ons", [1, 128])
        psb = [p.psum(f"psb{i}", [128, 512]) for i in range(8)]
        ptr = psb[0][:, 0:128]
        p.dma("sync", eyes, eye)
        p.dma("sync", ons, ones)
        for g in range(4):
            wv = p.mark()
            for kc in range(8):
                p.dma(["sync", "gpsimd", "scalar"][kc % 3],
                      wub[:, kc * F:(kc + 1) * F], wu[g, kc * 128:(kc + 1) * 128, :],
                      wait=wv)
            p.dma("sync", bus, bu[g], wait=wv)
            for tg in range(ntseg):
                wv = p.mark()
                for kc in range(8):
                    p.dma(["sync", "gpsimd", "scalar"][kc % 3],
                          xbuf[:, kc * TSEG:(kc + 1) * TSEG],
                          xT[kc * 128:(kc + 1) * 128, tg * TSEG:(tg + 1) * TSEG],
                          wait=wv)
                for fg in range(3):
                    nmb = 8 if fg < 2 else 6
                    for kc in range(8):
                        for mb in range(nmb):
                            fb = fg * 8 + mb
                            fw = min(128, F - fb * 128)
                            mm(p, psb[mb][:fw, 0:TSEG],
                               wub[:, kc * F + fb * 128:kc * F + fb * 128 + fw],
                               xbuf[:, kc * TSEG:(kc + 1) * TSEG],
                               kc == 0, kc == 7)
                    for mb in range(nmb):
                        fb = fg * 8 + mb
                        fw = min(128, F - fb * 128)
                        ts(p, mbuf[:fw, :], psb[mb][:fw, 0:TSEG],
                           bus[:fw, fb:fb + 1], "add")
                        act(p, mbuf[:fw, :], mbuf[:fw, :], AF.Silu)
                        p.dma("gpsimd", midT[g, fb * 128:fb * 128 + fw,
                                             tg * TSEG:(tg + 1) * TSEG],
                              mbuf[:fw, :])
        for e in range(8):
            g = e // 2
            wv = p.mark()
            qs = ["sync", "gpsimd", "scalar"]
            for fb in range(NF):
                fw = min(128, F - fb * 128)
                p.dma(qs[fb % 3], wub[:fw, fb * DM:fb * DM + DM],
                      wd[e, fb * 128:fb * 128 + fw, :], wait=wv)
            p.dma("sync", bds, bd[e], wait=wv)
            p.dma("sync", ce, cT[e:e + 1, :], wait=wv)
            for tg in range(ntseg):
                mm(p, psb[0][:, 0:TSEG], ons,
                   ce[0:1, tg * TSEG:(tg + 1) * TSEG], True, True)
                copy(p, crep, psb[0][:, 0:TSEG])
                for fb in range(NF):
                    fw = min(128, F - fb * 128)
                    wv = p.mark()
                    p.dma(qs[fb % 3], mbuf[:fw, :],
                          midT[g, fb * 128:fb * 128 + fw,
                               tg * TSEG:(tg + 1) * TSEG], wait=wv)
                    for mb in range(8):
                        mm(p, psb[mb][:, 0:TSEG],
                           wub[:fw, fb * DM + mb * 128:fb * DM + (mb + 1) * 128],
                           mbuf[:fw, :], fb == 0, fb == NF - 1)
                for mb in range(8):
                    ts(p, ob, psb[mb][:, 0:TSEG],
                       bds[:, mb:mb + 1], "add")
                    tt(p, ob, ob, crep, "mult")
                    if e == 0:
                        copy(p, yT_sb[:, mb * TPC + tg * TSEG:
                                      mb * TPC + (tg + 1) * TSEG], ob)
                    else:
                        tt(p, yT_sb[:, mb * TPC + tg * TSEG:
                                    mb * TPC + (tg + 1) * TSEG],
                           yT_sb[:, mb * TPC + tg * TSEG:
                                 mb * TPC + (tg + 1) * TSEG], ob, "add")
        for t in range(TPC // 128):
            wv = p.mark()
            p.dma("sync", xt, x[t * 128:(t + 1) * 128, :], wait=wv)
            p.dma("gpsimd", st, s[t * 128:(t + 1) * 128, :], wait=wv)
            for mb in range(8):
                trans(p, ptr, yT_sb[:, mb * TPC + t * 128:
                                    mb * TPC + (t + 1) * 128], eyes)
                copy(p, yn[:, mb * 128:(mb + 1) * 128], ptr)
            act(p, yn, yn, AF.Silu)
            tt(p, yn, yn, st, "mult")
            tt(p, yn, yn, xt, "add")
            p.dma("gpsimd", y[t * 128:(t + 1) * 128, :], yn)
        return p
    return cached(("moe2",), b)


def launch_moe2(xT, inp, c, x, s):
    p = build_moe2()
    F = 2730
    bu = np.zeros((4, 128, 22), np.float32)
    for g in range(4):
        bv = np.asarray(inp["L2_ffn_b_up"][g])
        for fb in range(22):
            seg = bv[fb * 128:(fb + 1) * 128]
            bu[g, :len(seg), fb] = seg
    bd = np.zeros((8, 128, 8), np.float32)
    for e in range(8):
        bd[e] = np.asarray(inp["L2_ffn_b_down"][e]).reshape(8, 128).T
    cT = np.ascontiguousarray(np.asarray(c, np.float32).T)
    wu = np.ascontiguousarray(np.asarray(inp["L2_ffn_W_up"], np.float32))
    wd = np.ascontiguousarray(np.asarray(inp["L2_ffn_W_down"], np.float32))
    ims = []
    for ci in range(NCORE):
        sl = slice(ci * TPC, (ci + 1) * TPC)
        ims.append({"xT": np.ascontiguousarray(xT[:, sl]), "wu": wu, "bu": bu,
                    "wd": wd, "bd": bd,
                    "cT": np.ascontiguousarray(cT[:, sl]),
                    "x": np.ascontiguousarray(x[sl]),
                    "s": np.ascontiguousarray(s[sl]), "eye": _EYE,
                    "ones": _ONES_ROW})
    rs = run(p, ims)
    return np.concatenate([r["y"] for r in rs], 0)


def block0(x, inp):
    xn = launch_norm("ln", x)
    xnT = T(xn)
    wqkvT = np.concatenate([T(inp["L0_attn_Wq"]), T(inp["L0_attn_Wk"]),
                            T(inp["L0_attn_Wv"])], axis=1)
    qkvT = launch_gemm(xnT, wqkvT)
    o = launch_swa(qkvT[0:1024], qkvT[1024:2048], T(qkvT[2048:3072]))
    sT = launch_gemm(T(o), T(inp["L0_attn_Wo"]))
    hT = launch_gemm(xnT, T(inp["L0_ffn_w1"]), "gelu",
                     bias=inp["L0_ffn_b1"], wnat=inp["L0_ffn_w1"])
    fT = launch_gemm(hT, T(inp["L0_ffn_w2"]), "none",
                     bias=inp["L0_ffn_b2"], wnat=inp["L0_ffn_w2"])
    return launch_combine(x, T(sT), T(fT))


def block1(x, inp):
    xn = launch_norm("ln", x)
    xnT = T(xn)
    qkvT = launch_gemm(xnT, T(inp["L1_attn_w_qkv"]), "elu1", elu1_blocks=16)
    o = launch_linattn(qkvT[0:1024], T(qkvT[1024:2048]), T(qkvT[2048:3072]))
    sT = launch_gemm(T(o), T(inp["L1_attn_w_out"]), bias=inp["L1_attn_b_out"])
    c = launch_router("top2prob", xnT, T(inp["L1_ffn_router_w"]))
    return launch_moe_chain(xnT, inp["L1_ffn_w1"], inp["L1_ffn_w2"],
                            c, x, T(sT))


def block2(x, inp):
    xn = launch_norm("ln", x)
    xnT = T(xn)
    s2 = launch_causal(xnT, xnT, xn)
    Sm = launch_router("softmax", xnT, np.asarray(inp["L2_ffn_Wr"]),
                       br=inp["L2_ffn_br"])
    c = launch_bisect(Sm)
    return launch_moe2(xnT, inp, c, x, s2)


def block3(x, inp):
    xn = launch_norm("ln", x)
    xnT = T(xn)
    WT = np.concatenate([np.asarray(inp["L3_attn_Wq"]),
                         np.asarray(inp["L3_attn_Wk"]),
                         np.asarray(inp["L3_attn_Wv"]),
                         np.asarray(inp["L3_attn_Wg"]),
                         np.asarray(inp["L3_attn_Wgk"]),
                         np.asarray(inp["L3_attn_Wb"])], axis=1)
    bias = np.zeros(4632, np.float32)
    bias[4608:4620] = np.asarray(inp["L3_attn_dt_bias"])
    bias[4620:4632] = np.asarray(inp["L3_attn_bb"])
    projT = launch_gemm(xnT, WT, bias=bias)
    q = launch_conv(T(projT[0:768]), inp["L3_attn_conv_q"], True)
    k = launch_conv(T(projT[768:1536]), inp["L3_attn_conv_k"], True)
    v = launch_conv(T(projT[1536:3072]), inp["L3_attn_conv_v"], False)
    gproj = T(projT[3072:4608])
    gk, beta = launch_scanprep(T(projT[4608:4620]), T(projT[4620:4632]),
                               inp["L3_attn_A_log"])
    o = launch_scan(T(q), T(k), k, v, gk, beta)
    og = launch_opost(o, gproj, inp["L3_attn_gnorm_w"])
    s3T = launch_gemm(T(og), np.asarray(inp["L3_attn_Wo"]))
    a = launch_norm("rms", xn, w=inp["L3_ffn_attn_norm_w"], eps=1e-6)
    aT = T(a)
    qkvT = launch_gemm(aT, np.concatenate(
        [T(inp["L3_ffn_wq"]), T(inp["L3_ffn_wk"]), T(inp["L3_ffn_wv"])], 1))
    attn = launch_causal(qkvT[0:1024], qkvT[1024:1280],
                         T(qkvT[1280:1536]), kv_rep=4)
    awT = launch_gemm(T(attn), T(inp["L3_ffn_wo"]))
    h2 = launch_ew("add2", a=xn, b=T(awT))
    h2n = launch_norm("rms", h2, w=inp["L3_ffn_ffn_norm_w"], eps=1e-6)
    h2nT = T(h2n)
    cmb = launch_router("top2renorm", h2nT, T(inp["L3_ffn_gate_w"]))
    ewT = np.ascontiguousarray(
        np.transpose(np.asarray(inp["L3_ffn_expert_w"], np.float32), (0, 2, 1)))
    return launch_moe_chain(h2nT, ewT, None, cmb, x, T(s3T), base=h2)


def kernel(**inp):
    x = np.ascontiguousarray(np.asarray(inp["L0_attn_x"], np.float32)
                             .reshape(NTOK, DM))
    x = block0(x, inp)
    x = block1(x, inp)
    x = block2(x, inp)
    x = block3(x, inp)
    return x.reshape(4, 4096, DM)
